# version 11
# baseline (speedup 1.0000x reference)
import contextlib
import numpy as np
import concourse.bass as bass
import concourse.mybir as mybir
from concourse.bass_utils import run_bass_kernel_spmd

F32 = mybir.dt.float32
BF16 = mybir.dt.bfloat16
FP16 = mybir.dt.float16
I32 = mybir.dt.int32
I16 = mybir.dt.int16
AF = mybir.ActivationFunctionType
ALU = mybir.AluOpType

B, S, D = 16, 2048, 2048
NH, DK = 16, 128
MLP = 16384
NCORES = 8
BPC = B // NCORES
T = BPC * S
NTT = T // 128
NHT = D // 128
NGRP = MLP // 512
NSUP = MLP // 1024
INVSQ = float(1.0 / np.sqrt(128.0))
EPS = 1e-5

SUBSETS = [("elo", 0, 3, 0, 0), ("olo", 8, 11, 1, 0),
           ("ehi", 4, 7, 0, 1), ("ohi", 12, 15, 1, 1)]


def _pi(n):
    out = []
    for j in range(n // 256):
        base = j * 256
        out.extend(range(base, base + 256, 2))
        out.extend(range(base + 1, base + 256, 2))
    return np.array(out, dtype=np.int64)


PI = _pi(D)


def _pack_bytes(w_packed_i32):
    a = np.ascontiguousarray(np.asarray(w_packed_i32, dtype=np.int32))
    n, k = a.shape
    return a.view(np.uint8).reshape(n, 4 * k)[:, 0::4]


def _prep_gate_up(w_packed, scales):
    by = _pack_bytes(w_packed)
    byT = np.ascontiguousarray(by.T)
    slab = byT.reshape(8, 128, NGRP, 512).transpose(2, 1, 0, 3)
    pk = np.ascontiguousarray(slab).reshape(NGRP, 128, 8 * 512)
    pk = pk.view(np.int32).reshape(NGRP, 128, 1024)
    sT = np.ascontiguousarray(np.asarray(scales, np.float32).T)
    s4 = sT.reshape(8, 8, NGRP, 256, 2)
    sc = np.ascontiguousarray(s4.transpose(2, 1, 0, 4, 3)).reshape(NGRP, 8, 8, 512)
    return pk, sc


def _prep_down(w_packed, scales):
    by = _pack_bytes(w_packed)
    byT = np.ascontiguousarray(by.T)
    slab = byT.reshape(NSUP, 4, 128, 2048)
    pk = np.ascontiguousarray(slab).view(np.int32).reshape(NSUP, 4, 128, 512)
    sT = np.ascontiguousarray(np.asarray(scales, np.float32).T)
    s4 = sT.reshape(NSUP, 4, 8, 1024, 2)
    sc = np.ascontiguousarray(s4.transpose(0, 2, 1, 4, 3)).reshape(NSUP, 8, 4, 2048)
    return pk, sc


def host_prep(inputs):
    x = np.ascontiguousarray(np.asarray(inputs["L0_attn_x"], dtype=np.float32))
    wqkv = np.ascontiguousarray(
        np.asarray(inputs["L0_attn_W_qkv"], np.float32)[:, PI])
    wout = np.ascontiguousarray(
        np.asarray(inputs["L0_attn_W_out"], np.float32)[PI, :])
    gpk, gsc = _prep_gate_up(inputs["L0_ffn_gate_W_packed"],
                             inputs["L0_ffn_gate_scales"])
    upk, usc = _prep_gate_up(inputs["L0_ffn_up_W_packed"],
                             inputs["L0_ffn_up_scales"])
    dpk, dsc = _prep_down(inputs["L0_ffn_down_W_packed"],
                          inputs["L0_ffn_down_scales"])
    shared = {"wqkv": wqkv, "wout": wout, "gpk": gpk, "gsc": gsc,
              "upk": upk, "usc": usc, "dpk": dpk, "dsc": dsc}
    in_maps = []
    for c in range(NCORES):
        m = dict(shared)
        m["x"] = np.ascontiguousarray(x[BPC * c: BPC * (c + 1)].reshape(T, D))
        in_maps.append(m)
    return in_maps


class Ctx:
    pass


def build_program(upto_phase=3, taps=()):
    C = Ctx()
    nc = bass.Bass("TRN2", target_bir_lowering=False)
    C.nc = nc
    phases = [0, 1, "2a", "2b", 3]
    upto = phases.index(upto_phase)

    x_d = nc.dram_tensor("x", [T, D], F32, kind="ExternalInput").ap()
    wqkv_d = nc.dram_tensor("wqkv", [3 * D, D], F32, kind="ExternalInput").ap()
    wout_d = nc.dram_tensor("wout", [D, D], F32, kind="ExternalInput").ap()
    gpk_d = nc.dram_tensor("gpk", [NGRP, 128, 1024], I32, kind="ExternalInput").ap()
    gsc_d = nc.dram_tensor("gsc", [NGRP, 8, 8, 512], F32, kind="ExternalInput").ap()
    upk_d = nc.dram_tensor("upk", [NGRP, 128, 1024], I32, kind="ExternalInput").ap()
    usc_d = nc.dram_tensor("usc", [NGRP, 8, 8, 512], F32, kind="ExternalInput").ap()
    dpk_d = nc.dram_tensor("dpk", [NSUP, 4, 128, 512], I32, kind="ExternalInput").ap()
    dsc_d = nc.dram_tensor("dsc", [NSUP, 8, 4, 2048], F32, kind="ExternalInput").ap()
    y_d = nc.dram_tensor("y", [T, D], F32, kind="ExternalOutput").ap()
    kind = lambda n: "ExternalOutput" if n in taps else "Internal"
    aoT_d = nc.dram_tensor("aoT", [BPC, NH, 128, S], BF16, kind=kind("aoT")).ap()
    gl_d = nc.dram_tensor("intT", [NSUP, 8, 8, 128, 512], BF16,
                          kind=kind("intT")).ap()
    up_d = nc.dram_tensor("intU", [NSUP, 8, 8, 128, 512], BF16,
                          kind=kind("intT")).ap()
    xnT_tap = (nc.dram_tensor("xnT_t", [NHT, 128, T], BF16,
                              kind="ExternalOutput").ap() if "xnT" in taps else None)
    stat_tap = (nc.dram_tensor("stat_t", [128, 256], F32,
                               kind="ExternalOutput").ap() if "stat" in taps else None)
    C.stat_tap = stat_tap
    acc_tap = (nc.dram_tensor("acc_t", [NTT, 128, D], FP16,
                              kind="ExternalOutput").ap() if "acc" in taps else None)

    with contextlib.ExitStack() as top:
        ent = top.enter_context
        names = ["consts", "ph", "dbg",
                 "p0_x0", "p0_x1", "p0_x2", "p0_stat", "p0_act", "p0_norm",
                 "p0_tr", "p0_xnT",
                 "a_w0", "a_w1", "a_wtr", "a_wT", "a_qkv", "a_qev", "a_vtr",
                 "a_vn", "a_s", "a_exp", "a_msk", "a_pv", "a_oev",
                 "a_ao0", "a_ao1",
                 "i_wo0", "i_wo1", "i_wtr", "i_woT", "i_ao0", "i_ao1",
                 "i_mm", "i_ev",
                 "g_pk", "g_pk1", "g_sc0", "g_sc1", "g_ext", "g_act",
                 "g_w0",
                 "g_mm", "g_gelu", "g_upc", "g_ist0", "g_ist1",
                 "g_ist2", "g_ist3", "g_ist4", "g_ist5", "g_ist6", "g_ist7",
                 "g_ch",
                 "d_pk0", "d_pk1", "d_sc0", "d_ext",
                 "d_act", "d_it", "d_w", "d_il0", "d_il1", "d_il2", "d_il3",
                 "d_mm", "d_acc", "d_ch",
                 "p3_x0", "p3_x1", "p3_out", "p3_st0", "p3_st1"]
        sh, scnt = {}, {}
        for n in names:
            sh[n] = ent(nc.semaphore(n))
            scnt[n] = 0
        C.sh, C.scnt = sh, scnt

        def inc(name, instr, k=1):
            instr.then_inc(sh[name], k)
            scnt[name] += k

        def chk(name, val):
            assert scnt[name] == val, f"{name}: {scnt[name]} != {val}"
        C.inc, C.chk = inc, chk

        ident_f32 = ent(nc.sbuf_tensor("ident_f32", [128, 128], F32))
        ident_bf = ent(nc.sbuf_tensor("ident_bf", [128, 128], BF16))
        ones_bf = ent(nc.sbuf_tensor("ones_bf", [128, 128], BF16))
        cpool = ent(nc.sbuf_tensor("cpool", [128, 4], F32))
        C.NEG2, C.NEG6, C.EPSC = cpool[:, 0:1], cpool[:, 1:2], cpool[:, 2:3]
        C.ident_f32, C.ident_bf = ident_f32, ident_bf
        C.ones_bf = ones_bf

        block = ent(nc.Block())
        C.block = block

        @block.gpsimd
        def _(g):
            g.memset(ident_f32[:], 1.0)
            g.memset(ones_bf[:], 1.0)
            g.memset(cpool[:, 0:1], -2.0)
            g.memset(cpool[:, 1:2], -6.0)
            inc("consts", g.memset(cpool[:, 2:3], EPS), 1)
            g.wait_ge(sh["consts"], 1)
            for dst, src_ in ((ident_f32, ident_f32), (ident_bf, ones_bf)):
                g.affine_select(dst[:], src_[:], pattern=[[-1, 128]],
                                compare_op=ALU.is_equal, fill=0.0,
                                base=0, channel_multiplier=1)
            inc("consts", g.drain(), 1)

        for en in ("sync", "vector", "scalar", "tensor"):
            getattr(block, en)(lambda e: e.wait_ge(sh["consts"], 2))

        def barrier():
            for en in ("sync", "vector", "scalar", "tensor", "gpsimd"):
                getattr(block, en)(lambda e: inc("ph", e.drain(), 1))
            tgt = scnt["ph"]
            for en in ("sync", "vector", "scalar", "tensor", "gpsimd"):
                getattr(block, en)(lambda e: e.wait_ge(sh["ph"], tgt))
        C.barrier = barrier

        with contextlib.ExitStack() as rga:
            ea = rga.enter_context
            xnT = [ea(nc.sbuf_tensor(f"xnT{j}", [128, T], BF16))
                   for j in range(NHT)]
            _phase0(C, ea, x_d, xnT)
            if stat_tap is not None:
                @block.sync
                def _(sy):
                    sy.wait_ge(sh["p0_xnT"], NTT * NHT)
                    sy.dma_start(out=stat_tap, in_=C.stat_all_ref[:]
                                 ).then_inc(sh["dbg"], 16)
                    sy.wait_ge(sh["dbg"], scnt["dbg"] + 16)
                scnt["dbg"] += 16
            C.p0stack.close()
            if xnT_tap is not None:
                @block.sync
                def _(sy):
                    sy.wait_ge(sh["p0_xnT"], NTT * NHT)
                    for j in range(NHT):
                        sy.dma_start(out=xnT_tap[j], in_=xnT[j][:]
                                     ).then_inc(sh["dbg"], 16)
                    sy.wait_ge(sh["dbg"], scnt["dbg"] + 16 * NHT)
                scnt["dbg"] += 16 * NHT
            if upto >= 1:
                barrier()
                _phase1(C, ea, wqkv_d, aoT_d, xnT)
            if upto >= 2:
                barrier()
                _phase2a(C, ea, gpk_d, gsc_d, upk_d, usc_d, gl_d, up_d, xnT)
        if upto >= 3:
            with contextlib.ExitStack() as rgb:
                eb = rgb.enter_context
                acc = [eb(nc.sbuf_tensor(f"acc{j}", [128, D], FP16))
                       for j in range(NTT)]
                barrier()
                _phase2b(C, eb, dpk_d, dsc_d, gl_d, up_d, aoT_d, wout_d, acc)
                if acc_tap is not None:
                    @block.sync
                    def _(sy):
                        sy.wait_ge(sh["d_acc"], scnt["d_acc"])
                        base = scnt["dbg"]
                        for j in range(NTT):
                            sy.dma_start(out=acc_tap[j], in_=acc[j][:]
                                         ).then_inc(sh["dbg"], 16)
                        sy.wait_ge(sh["dbg"], base + 16 * NTT)
                    scnt["dbg"] += 16 * NTT
                if upto >= 4:
                    barrier()
                    _phase3(C, eb, x_d, y_d, acc)

    return nc


def _phase0(C, ea, x_d, xnT):
    nc, sh, block, inc, chk = C.nc, C.sh, C.block, C.inc, C.chk
    scnt = C.scnt
    p0stack = contextlib.ExitStack()
    e0 = p0stack.enter_context
    xst = [e0(nc.sbuf_tensor(f"p0_x{i}", [128, D], F32)) for i in range(3)]
    xnp = [e0(nc.sbuf_tensor(f"p0_xn{i}", [128, D], BF16)) for i in range(2)]
    xdv = e0(nc.sbuf_tensor("p0_dv", [128, D], BF16))
    xda = e0(nc.sbuf_tensor("p0_da", [128, D], BF16))
    stat_all = e0(nc.sbuf_tensor("p0_stat", [128, 256], F32))
    C.stat_all_ref = stat_all
    pstr = [e0(nc.psum_tensor(f"p0_ps{i}", [128, 1024], BF16)) for i in range(2)]

    a_cnt, sq_cnt, n_cnt, v_cnt, ex_cnt = {}, {}, {}, {}, {}

    def sc(i, k):
        return stat_all[:, 8 * i + k:8 * i + k + 1]

    def emit_accum(i):
        @block.vector
        def _(v):
            v.wait_ge(sh[f"p0_x{i % 3}"], 16 * (i // 3 + 1))
            if i >= 1:
                v.wait_ge(sh["p0_stat"], a_cnt[i - 1])
            inc("p0_stat",
                v.tensor_scalar(xdv[:], xst[i % 3][:], 1.0, None,
                                ALU.mult, ALU.add, accum_out=sc(i, 0)), 1)
        a_cnt[i] = scnt["p0_stat"]

    for i in range(NTT):
        @block.sync
        def _(sy):
            if i >= 3:
                sy.wait_ge(sh["p0_norm"], 2 * (i - 2))
            sy.dma_start(out=xst[i % 3][:], in_=x_d[128 * i:128 * (i + 1), :]
                         ).then_inc(sh[f"p0_x{i % 3}"], 16)
            C.scnt[f"p0_x{i % 3}"] += 16

        if i == 0:
            emit_accum(0)
            emit_accum(1)

        @block.scalar
        def _(s):
            s.wait_ge(sh["p0_stat"], a_cnt[i])
            if i >= 1:
                s.wait_ge(sh["p0_act"], sq_cnt[i - 1])
            inc("p0_act",
                s.activation(xda[:], xst[i % 3][:], AF.Square,
                             accum_out=sc(i, 1)), 1)
        sq_cnt[i] = scnt["p0_act"]

        @block.vector
        def _(v):
            v.wait_ge(sh["p0_stat"], a_cnt[i])
            inc("p0_stat",
                v.tensor_scalar(sc(i, 2), sc(i, 0), -1.0 / D, None,
                                ALU.mult, ALU.bypass), 1)
        n_cnt[i] = scnt["p0_stat"]

        @block.scalar
        def _(s):
            s.wait_ge(sh["p0_stat"], n_cnt[i])
            inc("p0_act", s.activation(sc(i, 3), sc(i, 2), AF.Square), 1)
        ms_cnt = scnt["p0_act"]

        @block.vector
        def _(v):
            v.wait_ge(sh["p0_act"], ms_cnt)
            inc("p0_stat",
                v.scalar_tensor_tensor(sc(i, 4), sc(i, 1), 1.0 / D, sc(i, 3),
                                       ALU.mult, ALU.subtract), 1)
        v_cnt[i] = scnt["p0_stat"]

        @block.scalar
        def _(s):
            s.wait_ge(sh["p0_stat"], v_cnt[i])
            inc("p0_act", s.activation(sc(i, 6), sc(i, 4), AF.Ln,
                                       bias=C.EPSC), 1)
            ln_c = scnt["p0_act"]
            s.wait_ge(sh["p0_act"], ln_c)
            inc("p0_act", s.activation(sc(i, 5), sc(i, 6), AF.Exp,
                                       scale=-0.5), 1)
        ex_cnt[i] = scnt["p0_act"]

        @block.vector
        def _(v):
            v.wait_ge(sh["p0_act"], ex_cnt[i])
            if i >= 2:
                v.wait_ge(sh["p0_tr"], NHT * (i - 1))
            src = xst[i % 3][:].rearrange("p (blk pair par) -> p blk pair par",
                                          blk=8, pair=128, par=2)
            dst = xnp[i % 2][:].rearrange("p (blk par pair) -> p blk par pair",
                                          blk=8, par=2, pair=128)
            for par in range(2):
                inc("p0_norm",
                    v.tensor_scalar(dst[:, :, par, :], src[:, :, :, par],
                                    sc(i, 2), sc(i, 5), ALU.add, ALU.mult), 1)

        if i + 2 < NTT:
            emit_accum(i + 2)

        @block.tensor
        def _(t):
            t.wait_ge(sh["p0_norm"], 2 * (i + 1))
            for j in range(NHT):
                ev = NHT * i + j
                if ev >= 2:
                    t.wait_ge(sh["p0_xnT"], ev - 1)
                inc("p0_tr",
                    t.transpose(pstr[ev % 2][:, 0:128],
                                xnp[i % 2][:, 128 * j:128 * (j + 1)],
                                C.ident_bf[:]), 1)

        @block.scalar
        def _(s):
            for j in range(NHT):
                ev = NHT * i + j
                s.wait_ge(sh["p0_tr"], ev + 1)
                inc("p0_xnT",
                    s.copy(xnT[j][:, 128 * i:128 * (i + 1)],
                           pstr[ev % 2][:, 0:128]), 1)

    assert (C.scnt["p0_x0"] + C.scnt["p0_x1"] + C.scnt["p0_x2"]
            == 16 * NTT)
    chk("p0_stat", 3 * NTT)
    chk("p0_act", 4 * NTT)
    chk("p0_norm", 2 * NTT)
    chk("p0_tr", NHT * NTT)
    chk("p0_xnT", NHT * NTT)
    C.p0stack = p0stack


def _phase1(C, ea, wqkv_d, aoT_d, xnT):
    nc, sh, scnt, block, inc, chk = C.nc, C.sh, C.scnt, C.block, C.inc, C.chk
    p1stack = contextlib.ExitStack()
    e1 = p1stack.enter_context
    wst = [e1(nc.sbuf_tensor(f"a_wst{i}", [128, D], F32)) for i in range(2)]
    psF = [e1(nc.psum_tensor(f"a_ps{i}", [128, 512], F32)) for i in range(6)]
    ptV = [e1(nc.psum_tensor(f"a_ptV{i}", [128, 1024], BF16)) for i in range(2)]

    with contextlib.ExitStack() as att:
        eat = att.enter_context
        wqkvT = eat(nc.sbuf_tensor("a_wqkvT", [128, 48 * 128], BF16))
        QT = eat(nc.sbuf_tensor("a_QT", [128, S], BF16))
        KT = eat(nc.sbuf_tensor("a_KT", [128, S], BF16))
        VT = eat(nc.sbuf_tensor("a_VT", [128, S], BF16))
        VN = eat(nc.sbuf_tensor("a_VN", [128, S], BF16))
        PT = [eat(nc.sbuf_tensor(f"a_PT{i}", [128, 512], BF16)) for i in range(4)]
        PTm = [eat(nc.sbuf_tensor(f"a_PTm{i}", [128, 512], BF16)) for i in range(4)]
        rl = eat(nc.sbuf_tensor("a_rl", [128, 512], F32))
        aost = [eat(nc.sbuf_tensor(f"a_ao{i}", [128, S], BF16)) for i in range(2)]

        def n_att_before(hb):
            return hb * 40

        for h in range(NH):
            @block.sync
            def _(sy):
                for w in range(3):
                    cnt = h * 3 + w
                    if cnt >= 2:
                        sy.wait_ge(sh["a_wtr"], NHT * (cnt - 1))
                    sy.dma_start(out=wst[cnt % 2][:],
                                 in_=wqkv_d[w * D + h * 128:w * D + h * 128 + 128, :]
                                 ).then_inc(sh[f"a_w{cnt % 2}"], 16)
                    scnt[f"a_w{cnt % 2}"] += 16

            @block.tensor
            def _(t):
                if h > 0:
                    t.wait_ge(sh["a_oev"], 2 * (h * BPC) * 4)
                for w in range(3):
                    cnt = h * 3 + w
                    t.wait_ge(sh[f"a_w{cnt % 2}"], 16 * (cnt // 2 + 1))
                    for j in range(NHT):
                        ev = NHT * cnt + j
                        if ev >= 2:
                            t.wait_ge(sh["a_wT"], ev - 1)
                        inc("a_wtr",
                            t.transpose(psF[4 + ev % 2][:, 0:128],
                                        wst[cnt % 2][:, 128 * j:128 * (j + 1)],
                                        C.ident_f32[:]), 1)

            @block.scalar
            def _(s):
                for w in range(3):
                    cnt = h * 3 + w
                    for j in range(NHT):
                        ev = NHT * cnt + j
                        s.wait_ge(sh["a_wtr"], ev + 1)
                        inc("a_wT",
                            s.copy(wqkvT[:, (j * 3 + w) * 128:(j * 3 + w) * 128 + 128],
                                   psF[4 + ev % 2][:, 0:128]), 1)

            for b in range(BPC):
                hb = h * BPC + b

                @block.tensor
                def _(t):
                    t.wait_ge(sh["a_wT"], 48 * (h + 1))
                    if hb > 0:
                        t.wait_ge(sh["a_pv"], n_att_before(hb))
                    for w in range(3):
                        t.wait_ge(sh["a_qev"], hb * 12 + 4 * w)
                        for ht in range(NHT):
                            for qb in range(4):
                                mm = t.matmul(
                                    psF[qb][:],
                                    wqkvT[:, (ht * 3 + w) * 128:(ht * 3 + w) * 128 + 128],
                                    xnT[ht][:, b * S + qb * 512:b * S + qb * 512 + 512],
                                    start=(ht == 0), stop=(ht == NHT - 1))
                                if ht == NHT - 1:
                                    inc("a_qkv", mm, 1)

                @block.scalar
                def _(s):
                    for w in range(3):
                        dstt = [QT, KT, VT][w]
                        for qb in range(4):
                            s.wait_ge(sh["a_qkv"], hb * 12 + w * 4 + qb + 1)
                            inc("a_qev",
                                s.copy(dstt[:, qb * 512:qb * 512 + 512],
                                       psF[qb][:]), 1)

                @block.tensor
                def _(t):
                    t.wait_ge(sh["a_qev"], hb * 12 + 12)
                    for kt in range(NHT):
                        ev = hb * NHT + kt
                        if ev >= 2:
                            t.wait_ge(sh["a_vn"], ev - 1)
                        inc("a_vtr",
                            t.transpose(ptV[ev % 2][:, 0:128],
                                        VT[:, 128 * kt:128 * kt + 128],
                                        C.ident_bf[:]), 1)

                @block.scalar
                def _(s):
                    for kt in range(NHT):
                        ev = hb * NHT + kt
                        s.wait_ge(sh["a_vtr"], ev + 1)
                        inc("a_vn",
                            s.copy(VN[:, 128 * kt:128 * kt + 128],
                                   ptV[ev % 2][:, 0:128]), 1)

                iS0 = n_att_before(hb)

                @block.tensor
                def _(t):
                    t.wait_ge(sh["a_vn"], hb * NHT + NHT)
                    iS = iS0
                    for qb in range(4):
                        nk = 4 * qb + 4

                        def emit_S(j, iS_j):
                            if iS_j >= 4:
                                t.wait_ge(sh["a_exp"], iS_j - 3)
                            inc("a_s",
                                t.matmul(psF[iS_j % 4][:],
                                         KT[:, 128 * j:128 * j + 128],
                                         QT[:, qb * 512:qb * 512 + 512],
                                         start=True, stop=True), 1)

                        for i in range(nk):
                            if i == 0:
                                for j in range(min(4, nk)):
                                    emit_S(j, iS + j)
                                t.wait_ge(sh["a_oev"], 2 * (hb * 4 + qb))
                            t.wait_ge(sh["a_msk"], iS + 1)
                            t.matmul(psF[4][:], C.ones_bf[:], PTm[iS % 4][:],
                                     start=(i == 0), stop=(i == nk - 1))
                            inc("a_pv",
                                t.matmul(psF[5][:], VN[:, 128 * i:128 * i + 128],
                                         PTm[iS % 4][:],
                                         start=(i == 0), stop=(i == nk - 1)), 1)
                            if i + 4 < nk:
                                emit_S(i + 4, iS + 4)
                            iS += 1

                @block.scalar
                def _(s):
                    iS = iS0
                    for qb in range(4):
                        for i in range(4 * qb + 4):
                            s.wait_ge(sh["a_s"], iS + 1)
                            if iS >= 4:
                                s.wait_ge(sh["a_msk"], iS - 3)
                            inc("a_exp",
                                s.activation(PT[iS % 4][:], psF[iS % 4][:],
                                             AF.Exp, scale=INVSQ), 1)
                            iS += 1

                @block.gpsimd
                def _(g):
                    iS = iS0
                    for qb in range(4):
                        for i in range(4 * qb + 4):
                            g.wait_ge(sh["a_exp"], iS + 1)
                            if iS >= 4:
                                g.wait_ge(sh["a_pv"], iS - 3)
                            inc("a_msk",
                                g.affine_select(PTm[iS % 4][:], PT[iS % 4][:],
                                                pattern=[[1, 512]],
                                                compare_op=ALU.is_ge, fill=0.0,
                                                base=-128 * (i - 4 * qb),
                                                channel_multiplier=-1), 1)
                            iS += 1

                @block.vector
                def _(v):
                    if hb >= 2:
                        v.wait_ge(sh[f"a_ao{b}"], 16 * h)
                    for qb in range(4):
                        v.wait_ge(sh["a_pv"],
                                  iS0 + sum(4 * q + 4 for q in range(qb + 1)))
                        inc("a_oev", v.reciprocal(rl[:], psF[4][:]), 1)
                        v.wait_ge(sh["a_oev"], 2 * (hb * 4 + qb) + 1)
                        inc("a_oev",
                            v.tensor_tensor(aost[b][:, qb * 512:qb * 512 + 512],
                                            psF[5][:], rl[:], ALU.mult), 1)

                @block.scalar
                def _(s):
                    s.wait_ge(sh["a_oev"], 2 * (hb * 4 + 4))
                    inc(f"a_ao{b}",
                        s.dma_start(out=aoT_d[b, h, :, :], in_=aost[b][:]), 16)

        @block.scalar
        def _(s):
            s.wait_ge(sh["a_ao0"], 16 * NH)
            s.wait_ge(sh["a_ao1"], 16 * NH)

        assert scnt["a_w0"] + scnt["a_w1"] == 16 * NH * 3
        chk("a_wtr", NH * 3 * NHT)
        chk("a_wT", NH * 3 * NHT)
        chk("a_qkv", NH * BPC * 12)
        chk("a_qev", NH * BPC * 12)
        chk("a_vtr", NH * BPC * NHT)
        chk("a_vn", NH * BPC * NHT)
        chk("a_s", NH * BPC * 40)
        chk("a_exp", NH * BPC * 40)
        chk("a_msk", NH * BPC * 40)
        chk("a_pv", NH * BPC * 40)
        chk("a_oev", NH * BPC * 8)
        chk("a_ao0", 16 * NH)
        chk("a_ao1", 16 * NH)
    p1stack.close()


def _phase2a(C, ea, gpk_d, gsc_d, upk_d, usc_d, gl_d, up_d, xnT):
    nc, sh, scnt, block, inc, chk = C.nc, C.sh, C.scnt, C.block, C.inc, C.chk
    with contextlib.ExitStack() as p2a:
        e = p2a.enter_context
        pkg = [e(nc.sbuf_tensor(f"g_pk{i}", [128, 256], I32)) for i in range(2)]
        scgq = [e(nc.sbuf_tensor(f"g_scq{i}", [128, 1024], BF16))
                for i in range(2)]
        m32 = e(nc.sbuf_tensor("g_m32", [128, 256], I32))
        s32 = e(nc.sbuf_tensor("g_s32", [128, 256], I32))
        r1 = e(nc.sbuf_tensor("g_r1", [128, 512], BF16))
        sgn = e(nc.sbuf_tensor("g_sgn", [128, 512], BF16))
        tt1 = e(nc.sbuf_tensor("g_t1", [128, 512], BF16))
        w2 = [[e(nc.sbuf_tensor(f"w{p}_{m}", [128, 8192], BF16))
               for m in range(2)] for p in range(2)]
        glb = [e(nc.sbuf_tensor(f"g_gl{i}", [128, 512], BF16)) for i in range(3)]
        psGU = [e(nc.psum_tensor(f"g_ps{i}", [128, 512], F32)) for i in range(8)]

        for g in range(NGRP):
            for mat in range(2):
                li = 2 * g + mat

                @block.sync
                def _(sy):
                    for q in range(4):
                        gqi = 4 * li + q
                        if gqi >= 2:
                            sy.wait_ge(sh["g_ext"], 8 * (gqi - 1))
                        pks = "g_pk" if gqi % 2 == 0 else "g_pk1"
                        sy.dma_start(
                            out=pkg[gqi % 2][:],
                            in_=(gpk_d if mat == 0 else upk_d)[
                                g][:, q * 256:q * 256 + 256]
                        ).then_inc(sh[pks], 16)
                        scnt[pks] += 16

                @block.gpsimd
                def _(gq):
                    for q in range(4):
                        gqi = 4 * li + q
                        if gqi >= 2:
                            gq.wait_ge(sh["g_ch"], 20 * (gqi - 1))
                        srep = (gsc_d if mat == 0 else usc_d)[g][
                            :, None, 2 * q:2 * q + 2, :].broadcast_to(
                            [8, 16, 2, 512])
                        scs = "g_sc0" if gqi % 2 == 0 else "g_sc1"
                        gq.dma_start(out=scgq[gqi % 2][:], in_=srep
                                     ).then_inc(sh[scs], 16)
                        scnt[scs] += 16

                @block.vector
                def _(v):
                    if g >= 2:
                        v.wait_ge(sh["g_mm"], 64 * (g - 1))
                    for q in range(4):
                        gqi = 4 * li + q
                        v.wait_ge(sh["g_pk" if gqi % 2 == 0 else "g_pk1"],
                                  16 * (gqi // 2 + 1))
                        v.wait_ge(sh["g_sc0" if gqi % 2 == 0 else "g_sc1"],
                                  16 * (gqi // 2 + 1))
                        scv2 = scgq[gqi % 2][:].rearrange(
                            "p (jc x) -> p jc x", jc=2, x=512)
                        qsl = pkg[gqi % 2][:]
                        for si, (nm, msh, ssh, beta, nu) in enumerate(SUBSETS):
                            idx = li * 16 + q * 4 + si
                            v.tensor_scalar(m32[:], qsl, msh, 0x00070007,
                                            ALU.logical_shift_right,
                                            ALU.bitwise_and)
                            inc("g_ext",
                                v.tensor_scalar(s32[:], qsl, ssh, 0x00010001,
                                                ALU.logical_shift_right,
                                                ALU.bitwise_and), 2)
                            m16 = m32[:].bitcast(I16)
                            s16 = s32[:].bitcast(I16)
                            v.wait_ge(sh["g_ext"], 2 * idx + 2)
                            inc("g_ch",
                                v.tensor_scalar(r1[:], m16, 2.0, -8.0,
                                                ALU.mult, ALU.add), 1)
                            v.wait_ge(sh["g_ch"], 5 * idx + 1)
                            inc("g_ch",
                                v.scalar_tensor_tensor(tt1[:], m16, -2.0,
                                                       r1[:], ALU.add,
                                                       ALU.max), 1)
                            v.wait_ge(sh["g_ch"], 5 * idx + 2)
                            inc("g_ch",
                                v.scalar_tensor_tensor(r1[:], m16, 0.5,
                                                       tt1[:], ALU.mult,
                                                       ALU.max), 1)
                            inc("g_ch",
                                v.tensor_scalar(sgn[:], s16, -2.0, 1.0,
                                                ALU.mult, ALU.add), 1)
                            sg2v = sgn[:].rearrange("p (a b) -> p a b",
                                                    a=2, b=256)
                            scv = scv2[:, :, beta * 256:beta * 256 + 256]
                            v.wait_ge(sh["g_ch"], 5 * idx + 4)
                            inc("g_ch",
                                v.tensor_tensor(sg2v, sg2v, scv, ALU.mult), 1)
                            v.wait_ge(sh["g_ch"], 5 * idx + 5)
                            dst = w2[g % 2][mat][:].rearrange(
                                "p (ht k bb c) -> p ht bb k c",
                                ht=16, k=2, bb=2, c=128)
                            for j2 in range(2):
                                ht = 2 * (2 * q + j2) + nu
                                for k in range(2):
                                    o0 = j2 * 256 + k * 128
                                    inc("g_w0", v.tensor_tensor(
                                        dst[:, ht, beta, k, :],
                                        r1[:, o0:o0 + 128],
                                        sgn[:, o0:o0 + 128],
                                        ALU.mult), 1)

            for sr in range(4):
                gbase = (g * 4 + sr) * 8
                sig4 = g * 4 + sr
                sup, srel = divmod(sig4, 8)

                @block.tensor
                def _(t):
                    for half in range(2):
                        for mat in range(2):
                            t.wait_ge(sh["g_w0"], 64 * (2 * g + mat + 1))
                            ww = w2[g % 2][mat]
                            for ht in range(NHT):
                                for tq in range(4):
                                    tb = half * 4 + tq
                                    bank = mat * 4 + tq
                                    u = sig4 * 2 + half
                                    if ht == 0 and u >= 1:
                                        if mat == 0:
                                            t.wait_ge(sh["g_gelu"],
                                                      4 * (u - 1) + tq + 1)
                                        else:
                                            t.wait_ge(sh["g_upc"],
                                                      4 * (u - 1) + tq + 1)
                                    mm = t.matmul(
                                        psGU[bank][:],
                                        ww[:, (ht * 4 + sr) * 128:
                                           (ht * 4 + sr) * 128 + 128],
                                        xnT[ht][:, tb * 512:tb * 512 + 512],
                                        start=(ht == 0), stop=(ht == NHT - 1))
                                    if ht == NHT - 1:
                                        inc("g_mm", mm, 1)

                @block.scalar
                def _(s):
                    for half in range(2):
                        for mat in range(2):
                            for tq in range(4):
                                tb = half * 4 + tq
                                bank = mat * 4 + tq
                                nev = sig4 * 16 + half * 8 + mat * 4 + tq
                                s.wait_ge(sh["g_mm"], nev + 1)
                                if nev >= 3:
                                    s.wait_ge(sh[f"g_ist{nev % 3}"],
                                              16 * (nev // 3))
                                if mat == 0:
                                    inc("g_gelu",
                                        s.activation(glb[nev % 3][:],
                                                     psGU[bank][:],
                                                     AF.Gelu_apprx_tanh), 1)
                                    s.wait_ge(sh["g_gelu"],
                                              sig4 * 8 + half * 4 + tq + 1)
                                    inc(f"g_ist{nev % 3}",
                                        s.dma_start(
                                            out=gl_d[sup, tb, srel, :, :],
                                            in_=glb[nev % 3][:]), 16)
                                else:
                                    inc("g_upc",
                                        s.copy(glb[nev % 3][:],
                                               psGU[bank][:]), 1)
                                    s.wait_ge(sh["g_upc"],
                                              sig4 * 8 + half * 4 + tq + 1)
                                    inc(f"g_ist{nev % 3}",
                                        s.dma_start(
                                            out=up_d[sup, tb, srel, :, :],
                                            in_=glb[nev % 3][:]), 16)

        @block.scalar
        def _(s):
            for k in range(3):
                s.wait_ge(sh[f"g_ist{k}"], scnt[f"g_ist{k}"])
            s.wait_ge(sh["g_upc"], scnt["g_upc"])

        assert scnt["g_pk"] + scnt["g_pk1"] == 16 * NGRP * 8
        assert scnt["g_sc0"] + scnt["g_sc1"] == 16 * NGRP * 8
        chk("g_ext", NGRP * 2 * 32)
        chk("g_ch", NGRP * 2 * 80)
        chk("g_w0", NGRP * 2 * 64)
        chk("g_mm", NGRP * 64)
        chk("g_gelu", NGRP * 32)
        chk("g_upc", NGRP * 32)
        assert sum(scnt[f"g_ist{k}"] for k in range(3)) == 16 * NGRP * 64


def _phase2b(C, eb, dpk_d, dsc_d, gl_d, up_d, aoT_d, wout_d, acc):
    nc, sh, scnt, block, inc, chk = C.nc, C.sh, C.scnt, C.block, C.inc, C.chk

    with contextlib.ExitStack() as prj:
        ep_ = prj.enter_context
        wst = [ep_(nc.sbuf_tensor(f"i_wst{i}", [128, D], F32)) for i in range(2)]
        woT = [ep_(nc.sbuf_tensor(f"i_woT{j}", [128, D], BF16)) for j in range(8)]
        aosr = [ep_(nc.sbuf_tensor(f"i_ao{i}", [128, 8 * 128], BF16))
                for i in range(2)]
        psJ = [ep_(nc.psum_tensor(f"i_ps{i}", [128, 512], F32)) for i in range(8)]

        for half in range(2):
            @block.sync
            def _(sy):
                for et in range(NHT):
                    cnt = half * NHT + et
                    if cnt >= 2:
                        sy.wait_ge(sh["i_wtr"], 8 * (cnt - 1))
                    sy.dma_start(out=wst[cnt % 2][:],
                                 in_=wout_d[et * 128:et * 128 + 128, :]
                                 ).then_inc(sh[f"i_wo{cnt % 2}"], 16)
                    scnt[f"i_wo{cnt % 2}"] += 16

            @block.tensor
            def _(t):
                if half == 1:
                    t.wait_ge(sh["i_ev"], 4 * 32)
                for et in range(NHT):
                    cnt = half * NHT + et
                    t.wait_ge(sh[f"i_wo{cnt % 2}"], 16 * (cnt // 2 + 1))
                    for jl in range(8):
                        ev = 8 * cnt + jl
                        if ev >= 2:
                            t.wait_ge(sh["i_woT"], ev - 1)
                        inc("i_wtr",
                            t.transpose(psJ[ev % 2][:, 0:128],
                                        wst[cnt % 2][:, (half * 8 + jl) * 128:
                                                      (half * 8 + jl) * 128 + 128],
                                        C.ident_f32[:]), 1)

            @block.scalar
            def _(s):
                for et in range(NHT):
                    cnt = half * NHT + et
                    for jl in range(8):
                        ev = 8 * cnt + jl
                        s.wait_ge(sh["i_wtr"], ev + 1)
                        inc("i_woT",
                            s.copy(woT[jl][:, et * 128:et * 128 + 128],
                                   psJ[ev % 2][:, 0:128]), 1)

            for tt in range(NTT):
                li = half * NTT + tt
                b, ts_ = tt // 16, (tt % 16) * 128

                @block.sync
                def _(sy):
                    if li >= 2:
                        sy.wait_ge(sh["i_mm"], li - 1)
                    sy.dma_start(
                        out=aosr[li % 2][:],
                        in_=aoT_d[b, half * 8:half * 8 + 8, :, ts_:ts_ + 128]
                        .rearrange("hh p c -> p hh c")
                    ).then_inc(sh[f"i_ao{li % 2}"], 16)
                    scnt[f"i_ao{li % 2}"] += 16

                pset = [psJ[4 * (tt % 2) + ch] for ch in range(4)]

                @block.tensor
                def _(t):
                    if tt == 0:
                        t.wait_ge(sh["i_woT"], 128 * (half + 1))
                    t.wait_ge(sh[f"i_ao{li % 2}"], 16 * (li // 2 + 1))
                    if li >= 2:
                        t.wait_ge(sh["i_ev"], 4 * (li - 1))
                    for dtl in range(8):
                        for ch in range(4):
                            mm = t.matmul(
                                pset[ch][:],
                                aosr[li % 2][:, dtl * 128:dtl * 128 + 128],
                                woT[dtl][:, ch * 512:ch * 512 + 512],
                                start=(dtl == 0), stop=(dtl == 7))
                    inc("i_mm", mm, 1)

                if half == 0:
                    @block.scalar
                    def _(s):
                        s.wait_ge(sh["i_mm"], li + 1)
                        for ch in range(4):
                            inc("i_ev",
                                s.copy(acc[tt][:, ch * 512:ch * 512 + 512],
                                       pset[ch][:]), 1)
                else:
                    @block.vector
                    def _(v):
                        v.wait_ge(sh["i_mm"], li + 1)
                        for ch in range(4):
                            inc("i_ev", v.tensor_tensor(
                                acc[tt][:, ch * 512:ch * 512 + 512],
                                pset[ch][:],
                                acc[tt][:, ch * 512:ch * 512 + 512],
                                ALU.add), 1)

        assert scnt["i_wo0"] + scnt["i_wo1"] == 16 * 2 * NHT
        chk("i_wtr", 2 * NHT * 8)
        chk("i_woT", 2 * NHT * 8)
        assert scnt["i_ao0"] + scnt["i_ao1"] == 16 * 2 * NTT
        chk("i_mm", 2 * NTT)
        chk("i_ev", 2 * NTT * 4)

    with contextlib.ExitStack() as p2b:
        e = p2b.enter_context
        for en in ("sync", "gpsimd"):
            getattr(block, en)(lambda q: q.wait_ge(sh["i_ev"], 2 * NTT * 4))
        pkd = e(nc.sbuf_tensor("d_pkb", [128, 512], I32))
        scb = e(nc.sbuf_tensor("d_scb", [128, 2048], BF16))
        m32 = e(nc.sbuf_tensor("d_m32", [128, 256], I32))
        s32 = e(nc.sbuf_tensor("d_s32", [128, 256], I32))
        r1 = e(nc.sbuf_tensor("d_r1", [128, 512], BF16))
        sgn = e(nc.sbuf_tensor("d_sgn", [128, 512], BF16))
        tt1 = e(nc.sbuf_tensor("d_t1", [128, 512], BF16))
        wd = [e(nc.sbuf_tensor(f"d_wd{i}", [128, 2048], BF16))
              for i in range(15)]
        glt = [e(nc.sbuf_tensor(f"d_gt{i}", [128, 1024], BF16)) for i in range(2)]
        upt = e(nc.sbuf_tensor("d_ut", [128, 1024], BF16))
        psD = [e(nc.psum_tensor(f"d_ps{i}", [128, 512], F32)) for i in range(8)]

        DORD = ([(jc, q2, si) for jc in range(3) for q2 in range(2)
                 for si in range(4)]
                + [(3, q2, si) for q2 in range(2) for si in (0, 1)]
                + [(3, q2, si) for q2 in range(2) for si in (2, 3)])
        assert len(DORD) == 32

        def d_loads(sup, jc):
            J = sup * 4 + jc

            @block.sync
            def _(sy):
                if J >= 1:
                    sy.wait_ge(sh["d_ext"], 16 * J)
                sy.dma_start(out=pkd[:], in_=dpk_d[sup, jc]
                             ).then_inc(sh["d_pk0"], 16)
            scnt["d_pk0"] += 16

            @block.gpsimd
            def _(gq):
                if J >= 1:
                    gq.wait_ge(sh["d_ch"], 40 * J)
                srep = dsc_d[sup, :, jc, :][:, None, :].broadcast_to(
                    [8, 16, 2048])
                gq.dma_start(out=scb[:], in_=srep).then_inc(sh["d_sc0"], 16)
            scnt["d_sc0"] += 16

        def d_deq(sup, pos):
            jc, q2, si = DORD[pos]
            J = sup * 4 + jc
            idx = sup * 32 + pos
            nm, msh, ssh, beta, nu = SUBSETS[si]
            srel = 2 * jc + nu
            slot = (8 * sup + srel) % 15

            @block.vector
            def _(v):
                v.wait_ge(sh["d_pk0"], 16 * (J + 1))
                v.wait_ge(sh["d_sc0"], 16 * (J + 1))
                qsl = pkd[:, q2 * 256:q2 * 256 + 256]
                v.tensor_scalar(m32[:], qsl, msh, 0x00070007,
                                ALU.logical_shift_right, ALU.bitwise_and)
                inc("d_ext",
                    v.tensor_scalar(s32[:], qsl, ssh, 0x00010001,
                                    ALU.logical_shift_right,
                                    ALU.bitwise_and), 2)
                m16 = m32[:].bitcast(I16)
                s16 = s32[:].bitcast(I16)
                v.wait_ge(sh["d_ext"], 2 * idx + 2)
                inc("d_ch",
                    v.tensor_scalar(r1[:], m16, 2.0, -8.0,
                                    ALU.mult, ALU.add), 1)
                v.wait_ge(sh["d_ch"], 5 * idx + 1)
                inc("d_ch",
                    v.scalar_tensor_tensor(tt1[:], m16, -2.0, r1[:],
                                           ALU.add, ALU.max), 1)
                v.wait_ge(sh["d_ch"], 5 * idx + 2)
                inc("d_ch",
                    v.scalar_tensor_tensor(r1[:], m16, 0.5, tt1[:],
                                           ALU.mult, ALU.max), 1)
                inc("d_ch",
                    v.tensor_scalar(sgn[:], s16, -2.0, 1.0,
                                    ALU.mult, ALU.add), 1)
                scv = scb[:, beta * 1024 + q2 * 512:
                          beta * 1024 + q2 * 512 + 512]
                v.wait_ge(sh["d_ch"], 5 * idx + 4)
                inc("d_ch", v.tensor_tensor(sgn[:], sgn[:], scv, ALU.mult), 1)
                v.wait_ge(sh["d_ch"], 5 * idx + 5)
                if srel == 7 and sup >= 1:
                    v.wait_ge(sh["d_mm"], 32 * sup)
                elif sup >= 2 and pos == 0:
                    v.wait_ge(sh["d_mm"], 32 * (sup - 1))
                dst = wd[slot][:].rearrange(
                    "p (q j par c) -> p q j par c", q=2, j=4, par=2, c=128)
                inc("d_w", v.tensor_tensor(
                    dst[:, q2, :, beta, :],
                    r1[:].rearrange("p (j c) -> p j c", j=4, c=128),
                    sgn[:].rearrange("p (j c) -> p j c", j=4, c=128),
                    ALU.mult), 1)

        for jc in range(4):
            d_loads(0, jc)
            for pos in range(8 * jc, 8 * jc + 8):
                d_deq(0, pos)

        for sup in range(NSUP):
            for tt in range(NTT):
                li2 = sup * NTT + tt
                tb, c0 = tt // 4, (tt % 4) * 128

                @block.sync
                def _(sy):
                    if li2 >= 2:
                        sy.wait_ge(sh["d_mm"], li2 - 1)
                    sy.dma_start(
                        out=glt[li2 % 2][:],
                        in_=gl_d[sup, tb, :, :, c0:c0 + 128].rearrange(
                            "ss p c -> p ss c")
                    ).then_inc(sh[f"d_il{li2 % 2}"], 16)
                    scnt[f"d_il{li2 % 2}"] += 16
                    if li2 >= 1:
                        sy.wait_ge(sh["d_it"], li2)
                    sy.dma_start(
                        out=upt[:],
                        in_=up_d[sup, tb, :, :, c0:c0 + 128].rearrange(
                            "ss p c -> p ss c")
                    ).then_inc(sh["d_il2"], 16)
                    scnt["d_il2"] += 16


                pset = [psD[4 * (tt % 2) + ch] for ch in range(4)]

                @block.tensor
                def _(t):
                    if tt == 0:
                        t.wait_ge(sh["d_w"], 32 * sup + 28)
                    t.wait_ge(sh["d_it"], li2 + 1)
                    if li2 >= 2:
                        t.wait_ge(sh["d_acc"], 4 * (li2 - 1))
                    for srel in range(8):
                        if tt == 0 and srel == 7:
                            t.wait_ge(sh["d_w"], 32 * (sup + 1))
                        for ch in range(4):
                            mm = t.matmul(
                                pset[ch][:],
                                glt[li2 % 2][:, srel * 128:srel * 128 + 128],
                                wd[(8 * sup + srel) % 15][:,
                                   ch * 512:ch * 512 + 512],
                                start=(srel == 0), stop=(srel == 7))
                    inc("d_mm", mm, 1)

                @block.vector
                def _(v):
                    v.wait_ge(sh[f"d_il{li2 % 2}"], 16 * (li2 // 2 + 1))
                    v.wait_ge(sh["d_il2"], 16 * (li2 + 1))
                    inc("d_it",
                        v.tensor_tensor(glt[li2 % 2][:], glt[li2 % 2][:],
                                        upt[:], ALU.mult), 1)
                    if li2 >= 1:
                        v.wait_ge(sh["d_mm"], li2)
                        tp = (li2 - 1) % NTT
                        ppre = [psD[4 * (tp % 2) + ch] for ch in range(4)]
                        for ch in range(4):
                            inc("d_acc", v.tensor_tensor(
                                acc[tp][:, ch * 512:ch * 512 + 512],
                                ppre[ch][:],
                                acc[tp][:, ch * 512:ch * 512 + 512],
                                ALU.add), 1)

                if sup + 1 < NSUP and tt % 8 == 0:
                    d_loads(sup + 1, tt // 8)
                if sup + 1 < NSUP and tt < 28:
                    d_deq(sup + 1, tt)

            if sup + 1 < NSUP:
                for pos in range(28, 32):
                    d_deq(sup + 1, pos)

        @block.vector
        def _(v):
            v.wait_ge(sh["d_mm"], NSUP * NTT)
            plast = [psD[4 * ((NTT - 1) % 2) + ch] for ch in range(4)]
            for ch in range(4):
                inc("d_acc", v.tensor_tensor(
                    acc[NTT - 1][:, ch * 512:ch * 512 + 512],
                    plast[ch][:],
                    acc[NTT - 1][:, ch * 512:ch * 512 + 512],
                    ALU.add), 1)


        chk("d_pk0", 16 * NSUP * 4)
        chk("d_sc0", 16 * NSUP * 4)
        chk("d_ext", NSUP * 64)
        chk("d_ch", NSUP * 160)
        chk("d_w", NSUP * 32)
        assert scnt["d_il0"] + scnt["d_il1"] == 16 * NSUP * NTT
        chk("d_il2", 16 * NSUP * NTT)
        chk("d_it", NSUP * NTT)
        chk("d_mm", NSUP * NTT)
        chk("d_acc", NSUP * NTT * 4)


def _phase3(C, eb, x_d, y_d, acc):
    nc, sh, scnt, block, inc, chk = C.nc, C.sh, C.scnt, C.block, C.inc, C.chk
    with contextlib.ExitStack() as p3:
        e = p3.enter_context
        xst = [e(nc.sbuf_tensor(f"p3_x{i}", [128, D], F32)) for i in range(2)]
        ost = [e(nc.sbuf_tensor(f"p3_o{i}", [128, D], F32)) for i in range(2)]

        for i in range(NTT):
            @block.sync
            def _(sy):
                if i >= 2:
                    sy.wait_ge(sh["p3_out"], i - 1)
                sy.dma_start(out=xst[i % 2][:], in_=x_d[128 * i:128 * (i + 1), :]
                             ).then_inc(sh[f"p3_x{i % 2}"], 16)
            scnt[f"p3_x{i % 2}"] += 16

            @block.vector
            def _(v):
                v.wait_ge(sh[f"p3_x{i % 2}"], 16 * (i // 2 + 1))
                if i >= 2:
                    v.wait_ge(sh[f"p3_st{i % 2}"], 16 * (i // 2))
                av = acc[i][:].rearrange("p (blk par pair) -> p blk pair par",
                                         blk=8, par=2, pair=128)
                xv = xst[i % 2][:].rearrange("p (blk pair par) -> p blk pair par",
                                             blk=8, pair=128, par=2)
                ov = ost[i % 2][:].rearrange("p (blk pair par) -> p blk pair par",
                                             blk=8, pair=128, par=2)
                inc("p3_out", v.tensor_tensor(ov, av, xv, ALU.add), 1)

            @block.scalar
            def _(s):
                s.wait_ge(sh["p3_out"], i + 1)
                inc(f"p3_st{i % 2}",
                    s.dma_start(out=y_d[128 * i:128 * (i + 1), :],
                                in_=ost[i % 2][:]), 16)

        @block.scalar
        def _(s):
            s.wait_ge(sh["p3_st0"], scnt["p3_st0"])
            s.wait_ge(sh["p3_st1"], scnt["p3_st1"])

        assert scnt["p3_x0"] + scnt["p3_x1"] == 16 * NTT
        chk("p3_out", NTT)
        assert scnt["p3_st0"] + scnt["p3_st1"] == 16 * NTT


_CACHE = {}


def kernel(**inputs):
    in_maps = host_prep(inputs)
    if "nc" not in _CACHE:
        _CACHE["nc"] = build_program(upto_phase=3)
    nc = _CACHE["nc"]
    res = run_bass_kernel_spmd(nc, in_maps, core_ids=list(range(NCORES)))
    out = np.empty((B, S, D), dtype=np.float32)
    for c in range(NCORES):
        out[BPC * c: BPC * (c + 1)] = np.asarray(
            res.results[c]["y"]).reshape(BPC, S, D)
    return out


# version 12 (final)
# speedup vs baseline: 1.0216x; 1.0216x over previous
import contextlib
import numpy as np
import concourse.bass as bass
import concourse.mybir as mybir
from concourse.bass_utils import run_bass_kernel_spmd

F32 = mybir.dt.float32
BF16 = mybir.dt.bfloat16
FP16 = mybir.dt.float16
I32 = mybir.dt.int32
I16 = mybir.dt.int16
AF = mybir.ActivationFunctionType
ALU = mybir.AluOpType

B, S, D = 16, 2048, 2048
NH, DK = 16, 128
MLP = 16384
NCORES = 8
BPC = B // NCORES
T = BPC * S
NTT = T // 128
NHT = D // 128
NGRP = MLP // 512
NSUP = MLP // 1024
INVSQ = float(1.0 / np.sqrt(128.0))
EPS = 1e-5

SUBSETS = [("elo", 0, 3, 0, 0), ("olo", 8, 11, 1, 0),
           ("ehi", 4, 7, 0, 1), ("ohi", 12, 15, 1, 1)]


def _pi(n):
    out = []
    for j in range(n // 256):
        base = j * 256
        out.extend(range(base, base + 256, 2))
        out.extend(range(base + 1, base + 256, 2))
    return np.array(out, dtype=np.int64)


PI = _pi(D)


def _pack_bytes(w_packed_i32):
    a = np.ascontiguousarray(np.asarray(w_packed_i32, dtype=np.int32))
    n, k = a.shape
    return a.view(np.uint8).reshape(n, 4 * k)[:, 0::4]


def _prep_gate_up(w_packed, scales):
    by = _pack_bytes(w_packed)
    byT = np.ascontiguousarray(by.T)
    slab = byT.reshape(8, 128, NGRP, 512).transpose(2, 1, 0, 3)
    pk = np.ascontiguousarray(slab).reshape(NGRP, 128, 8 * 512)
    pk = pk.view(np.int32).reshape(NGRP, 128, 1024)
    sT = np.ascontiguousarray(np.asarray(scales, np.float32).T)
    s4 = sT.reshape(8, 8, NGRP, 256, 2)
    sc = np.ascontiguousarray(s4.transpose(2, 1, 0, 4, 3)).reshape(NGRP, 8, 8, 512)
    return pk, sc


def _prep_down(w_packed, scales):
    by = _pack_bytes(w_packed)
    byT = np.ascontiguousarray(by.T)
    slab = byT.reshape(NSUP, 4, 128, 2048)
    pk = np.ascontiguousarray(slab).view(np.int32).reshape(NSUP, 4, 128, 512)
    sT = np.ascontiguousarray(np.asarray(scales, np.float32).T)
    s4 = sT.reshape(NSUP, 4, 8, 1024, 2)
    sc = np.ascontiguousarray(s4.transpose(0, 2, 1, 4, 3)).reshape(NSUP, 8, 4, 2048)
    return pk, sc


def host_prep(inputs):
    x = np.ascontiguousarray(np.asarray(inputs["L0_attn_x"], dtype=np.float32))
    wqkv = np.ascontiguousarray(
        np.asarray(inputs["L0_attn_W_qkv"], np.float32)[:, PI])
    wout = np.ascontiguousarray(
        np.asarray(inputs["L0_attn_W_out"], np.float32)[PI, :])
    gpk, gsc = _prep_gate_up(inputs["L0_ffn_gate_W_packed"],
                             inputs["L0_ffn_gate_scales"])
    upk, usc = _prep_gate_up(inputs["L0_ffn_up_W_packed"],
                             inputs["L0_ffn_up_scales"])
    dpk, dsc = _prep_down(inputs["L0_ffn_down_W_packed"],
                          inputs["L0_ffn_down_scales"])
    shared = {"wqkv": wqkv, "wout": wout, "gpk": gpk, "gsc": gsc,
              "upk": upk, "usc": usc, "dpk": dpk, "dsc": dsc}
    in_maps = []
    for c in range(NCORES):
        m = dict(shared)
        m["x"] = np.ascontiguousarray(x[BPC * c: BPC * (c + 1)].reshape(T, D))
        in_maps.append(m)
    return in_maps


class Ctx:
    pass


def build_program(upto_phase=3, taps=()):
    C = Ctx()
    nc = bass.Bass("TRN2", target_bir_lowering=False)
    C.nc = nc
    phases = [0, 1, "2a", "2b", 3]
    upto = phases.index(upto_phase)

    x_d = nc.dram_tensor("x", [T, D], F32, kind="ExternalInput").ap()
    wqkv_d = nc.dram_tensor("wqkv", [3 * D, D], F32, kind="ExternalInput").ap()
    wout_d = nc.dram_tensor("wout", [D, D], F32, kind="ExternalInput").ap()
    gpk_d = nc.dram_tensor("gpk", [NGRP, 128, 1024], I32, kind="ExternalInput").ap()
    gsc_d = nc.dram_tensor("gsc", [NGRP, 8, 8, 512], F32, kind="ExternalInput").ap()
    upk_d = nc.dram_tensor("upk", [NGRP, 128, 1024], I32, kind="ExternalInput").ap()
    usc_d = nc.dram_tensor("usc", [NGRP, 8, 8, 512], F32, kind="ExternalInput").ap()
    dpk_d = nc.dram_tensor("dpk", [NSUP, 4, 128, 512], I32, kind="ExternalInput").ap()
    dsc_d = nc.dram_tensor("dsc", [NSUP, 8, 4, 2048], F32, kind="ExternalInput").ap()
    y_d = nc.dram_tensor("y", [T, D], F32, kind="ExternalOutput").ap()
    kind = lambda n: "ExternalOutput" if n in taps else "Internal"
    aoT_d = nc.dram_tensor("aoT", [BPC, NH, 128, S], BF16, kind=kind("aoT")).ap()
    gl_d = nc.dram_tensor("intT", [NSUP, 8, 8, 128, 512], BF16,
                          kind=kind("intT")).ap()
    up_d = nc.dram_tensor("intU", [NSUP, 8, 8, 128, 512], BF16,
                          kind=kind("intT")).ap()
    xnT_tap = (nc.dram_tensor("xnT_t", [NHT, 128, T], BF16,
                              kind="ExternalOutput").ap() if "xnT" in taps else None)
    stat_tap = (nc.dram_tensor("stat_t", [128, 256], F32,
                               kind="ExternalOutput").ap() if "stat" in taps else None)
    C.stat_tap = stat_tap
    acc_tap = (nc.dram_tensor("acc_t", [NTT, 128, D], FP16,
                              kind="ExternalOutput").ap() if "acc" in taps else None)

    with contextlib.ExitStack() as top:
        ent = top.enter_context
        names = ["consts", "ph", "dbg",
                 "p0_x0", "p0_x1", "p0_x2", "p0_stat", "p0_act", "p0_norm",
                 "p0_tr", "p0_xnT",
                 "a_w0", "a_w1", "a_wtr", "a_wT", "a_qkv", "a_qev", "a_vtr",
                 "a_vn", "a_s", "a_exp", "a_msk", "a_pv", "a_oev",
                 "a_ao0", "a_ao1",
                 "i_wo0", "i_wo1", "i_wtr", "i_woT", "i_ao0", "i_ao1",
                 "i_mm", "i_ev",
                 "g_pk", "g_pk1", "g_sc0", "g_sc1", "g_ext", "g_act",
                 "g_w0",
                 "g_mm", "g_gelu", "g_upc", "g_ist0", "g_ist1",
                 "g_ist2", "g_ist3", "g_ist4", "g_ist5", "g_ist6", "g_ist7",
                 "g_ch",
                 "d_pk0", "d_pk1", "d_sc0", "d_ext",
                 "d_act", "d_it", "d_w", "d_il0", "d_il1", "d_il2", "d_il3",
                 "d_mm", "d_acc", "d_ch",
                 "p3_x0", "p3_x1", "p3_out", "p3_st0", "p3_st1"]
        sh, scnt = {}, {}
        for n in names:
            sh[n] = ent(nc.semaphore(n))
            scnt[n] = 0
        C.sh, C.scnt = sh, scnt

        def inc(name, instr, k=1):
            instr.then_inc(sh[name], k)
            scnt[name] += k

        def chk(name, val):
            assert scnt[name] == val, f"{name}: {scnt[name]} != {val}"
        C.inc, C.chk = inc, chk

        ident_f32 = ent(nc.sbuf_tensor("ident_f32", [128, 128], F32))
        ident_bf = ent(nc.sbuf_tensor("ident_bf", [128, 128], BF16))
        ones_bf = ent(nc.sbuf_tensor("ones_bf", [128, 128], BF16))
        cpool = ent(nc.sbuf_tensor("cpool", [128, 4], F32))
        C.NEG2, C.NEG6, C.EPSC = cpool[:, 0:1], cpool[:, 1:2], cpool[:, 2:3]
        C.ident_f32, C.ident_bf = ident_f32, ident_bf
        C.ones_bf = ones_bf

        block = ent(nc.Block())
        C.block = block

        @block.gpsimd
        def _(g):
            g.memset(ident_f32[:], 1.0)
            g.memset(ones_bf[:], 1.0)
            g.memset(cpool[:, 0:1], -2.0)
            g.memset(cpool[:, 1:2], -6.0)
            inc("consts", g.memset(cpool[:, 2:3], EPS), 1)
            g.wait_ge(sh["consts"], 1)
            for dst, src_ in ((ident_f32, ident_f32), (ident_bf, ones_bf)):
                g.affine_select(dst[:], src_[:], pattern=[[-1, 128]],
                                compare_op=ALU.is_equal, fill=0.0,
                                base=0, channel_multiplier=1)
            inc("consts", g.drain(), 1)

        for en in ("sync", "vector", "scalar", "tensor"):
            getattr(block, en)(lambda e: e.wait_ge(sh["consts"], 2))

        def barrier():
            for en in ("sync", "vector", "scalar", "tensor", "gpsimd"):
                getattr(block, en)(lambda e: inc("ph", e.drain(), 1))
            tgt = scnt["ph"]
            for en in ("sync", "vector", "scalar", "tensor", "gpsimd"):
                getattr(block, en)(lambda e: e.wait_ge(sh["ph"], tgt))
        C.barrier = barrier

        with contextlib.ExitStack() as rga:
            ea = rga.enter_context
            xnT = [ea(nc.sbuf_tensor(f"xnT{j}", [128, T], BF16))
                   for j in range(NHT)]
            _phase0(C, ea, x_d, xnT)
            if stat_tap is not None:
                @block.sync
                def _(sy):
                    sy.wait_ge(sh["p0_xnT"], NTT * NHT)
                    sy.dma_start(out=stat_tap, in_=C.stat_all_ref[:]
                                 ).then_inc(sh["dbg"], 16)
                    sy.wait_ge(sh["dbg"], scnt["dbg"] + 16)
                scnt["dbg"] += 16
            C.p0stack.close()
            if xnT_tap is not None:
                @block.sync
                def _(sy):
                    sy.wait_ge(sh["p0_xnT"], NTT * NHT)
                    for j in range(NHT):
                        sy.dma_start(out=xnT_tap[j], in_=xnT[j][:]
                                     ).then_inc(sh["dbg"], 16)
                    sy.wait_ge(sh["dbg"], scnt["dbg"] + 16 * NHT)
                scnt["dbg"] += 16 * NHT
            if upto >= 1:
                barrier()
                _phase1(C, ea, wqkv_d, aoT_d, xnT)
            if upto >= 2:
                barrier()
                _phase2a(C, ea, gpk_d, gsc_d, upk_d, usc_d, gl_d, up_d, xnT)
        if upto >= 3:
            with contextlib.ExitStack() as rgb:
                eb = rgb.enter_context
                acc = [eb(nc.sbuf_tensor(f"acc{j}", [128, D], FP16))
                       for j in range(NTT)]
                barrier()
                _phase2b(C, eb, dpk_d, dsc_d, gl_d, up_d, aoT_d, wout_d, acc)
                if acc_tap is not None:
                    @block.sync
                    def _(sy):
                        sy.wait_ge(sh["d_acc"], scnt["d_acc"])
                        base = scnt["dbg"]
                        for j in range(NTT):
                            sy.dma_start(out=acc_tap[j], in_=acc[j][:]
                                         ).then_inc(sh["dbg"], 16)
                        sy.wait_ge(sh["dbg"], base + 16 * NTT)
                    scnt["dbg"] += 16 * NTT
                if upto >= 4:
                    barrier()
                    _phase3(C, eb, x_d, y_d, acc)

    return nc


def _phase0(C, ea, x_d, xnT):
    nc, sh, block, inc, chk = C.nc, C.sh, C.block, C.inc, C.chk
    scnt = C.scnt
    p0stack = contextlib.ExitStack()
    e0 = p0stack.enter_context
    xst = [e0(nc.sbuf_tensor(f"p0_x{i}", [128, D], F32)) for i in range(3)]
    xnp = [e0(nc.sbuf_tensor(f"p0_xn{i}", [128, D], BF16)) for i in range(2)]
    xdv = e0(nc.sbuf_tensor("p0_dv", [128, D], BF16))
    xda = e0(nc.sbuf_tensor("p0_da", [128, D], BF16))
    stat_all = e0(nc.sbuf_tensor("p0_stat", [128, 256], F32))
    C.stat_all_ref = stat_all
    pstr = [e0(nc.psum_tensor(f"p0_ps{i}", [128, 1024], BF16)) for i in range(2)]

    a_cnt, sq_cnt, n_cnt, v_cnt, ex_cnt = {}, {}, {}, {}, {}

    def sc(i, k):
        return stat_all[:, 8 * i + k:8 * i + k + 1]

    def emit_accum(i):
        @block.vector
        def _(v):
            v.wait_ge(sh[f"p0_x{i % 3}"], 16 * (i // 3 + 1))
            if i >= 1:
                v.wait_ge(sh["p0_stat"], a_cnt[i - 1])
            inc("p0_stat",
                v.tensor_scalar(xdv[:], xst[i % 3][:], 1.0, None,
                                ALU.mult, ALU.add, accum_out=sc(i, 0)), 1)
        a_cnt[i] = scnt["p0_stat"]

    for i in range(NTT):
        @block.sync
        def _(sy):
            if i >= 3:
                sy.wait_ge(sh["p0_norm"], 2 * (i - 2))
            sy.dma_start(out=xst[i % 3][:], in_=x_d[128 * i:128 * (i + 1), :]
                         ).then_inc(sh[f"p0_x{i % 3}"], 16)
            C.scnt[f"p0_x{i % 3}"] += 16

        if i == 0:
            emit_accum(0)
            emit_accum(1)

        @block.scalar
        def _(s):
            s.wait_ge(sh["p0_stat"], a_cnt[i])
            if i >= 1:
                s.wait_ge(sh["p0_act"], sq_cnt[i - 1])
            inc("p0_act",
                s.activation(xda[:], xst[i % 3][:], AF.Square,
                             accum_out=sc(i, 1)), 1)
        sq_cnt[i] = scnt["p0_act"]

        @block.vector
        def _(v):
            v.wait_ge(sh["p0_stat"], a_cnt[i])
            inc("p0_stat",
                v.tensor_scalar(sc(i, 2), sc(i, 0), -1.0 / D, None,
                                ALU.mult, ALU.bypass), 1)
        n_cnt[i] = scnt["p0_stat"]

        @block.scalar
        def _(s):
            s.wait_ge(sh["p0_stat"], n_cnt[i])
            inc("p0_act", s.activation(sc(i, 3), sc(i, 2), AF.Square), 1)
        ms_cnt = scnt["p0_act"]

        @block.vector
        def _(v):
            v.wait_ge(sh["p0_act"], ms_cnt)
            inc("p0_stat",
                v.scalar_tensor_tensor(sc(i, 4), sc(i, 1), 1.0 / D, sc(i, 3),
                                       ALU.mult, ALU.subtract), 1)
        v_cnt[i] = scnt["p0_stat"]

        @block.scalar
        def _(s):
            s.wait_ge(sh["p0_stat"], v_cnt[i])
            inc("p0_act", s.activation(sc(i, 6), sc(i, 4), AF.Ln,
                                       bias=C.EPSC), 1)
            ln_c = scnt["p0_act"]
            s.wait_ge(sh["p0_act"], ln_c)
            inc("p0_act", s.activation(sc(i, 5), sc(i, 6), AF.Exp,
                                       scale=-0.5), 1)
        ex_cnt[i] = scnt["p0_act"]

        @block.vector
        def _(v):
            v.wait_ge(sh["p0_act"], ex_cnt[i])
            if i >= 2:
                v.wait_ge(sh["p0_tr"], NHT * (i - 1))
            src = xst[i % 3][:].rearrange("p (blk pair par) -> p blk pair par",
                                          blk=8, pair=128, par=2)
            dst = xnp[i % 2][:].rearrange("p (blk par pair) -> p blk par pair",
                                          blk=8, par=2, pair=128)
            for par in range(2):
                inc("p0_norm",
                    v.tensor_scalar(dst[:, :, par, :], src[:, :, :, par],
                                    sc(i, 2), sc(i, 5), ALU.add, ALU.mult), 1)

        if i + 2 < NTT:
            emit_accum(i + 2)

        @block.tensor
        def _(t):
            t.wait_ge(sh["p0_norm"], 2 * (i + 1))
            for j in range(NHT):
                ev = NHT * i + j
                if ev >= 2:
                    t.wait_ge(sh["p0_xnT"], ev - 1)
                inc("p0_tr",
                    t.transpose(pstr[ev % 2][:, 0:128],
                                xnp[i % 2][:, 128 * j:128 * (j + 1)],
                                C.ident_bf[:]), 1)

        @block.scalar
        def _(s):
            for j in range(NHT):
                ev = NHT * i + j
                s.wait_ge(sh["p0_tr"], ev + 1)
                inc("p0_xnT",
                    s.copy(xnT[j][:, 128 * i:128 * (i + 1)],
                           pstr[ev % 2][:, 0:128]), 1)

    assert (C.scnt["p0_x0"] + C.scnt["p0_x1"] + C.scnt["p0_x2"]
            == 16 * NTT)
    chk("p0_stat", 3 * NTT)
    chk("p0_act", 4 * NTT)
    chk("p0_norm", 2 * NTT)
    chk("p0_tr", NHT * NTT)
    chk("p0_xnT", NHT * NTT)
    C.p0stack = p0stack


def _phase1(C, ea, wqkv_d, aoT_d, xnT):
    nc, sh, scnt, block, inc, chk = C.nc, C.sh, C.scnt, C.block, C.inc, C.chk
    p1stack = contextlib.ExitStack()
    e1 = p1stack.enter_context
    wst = [e1(nc.sbuf_tensor(f"a_wst{i}", [128, D], F32)) for i in range(2)]
    psF = [e1(nc.psum_tensor(f"a_ps{i}", [128, 512], F32)) for i in range(6)]
    ptV = [e1(nc.psum_tensor(f"a_ptV{i}", [128, 1024], BF16)) for i in range(2)]

    with contextlib.ExitStack() as att:
        eat = att.enter_context
        wqkvT = eat(nc.sbuf_tensor("a_wqkvT", [128, 48 * 128], BF16))
        QT = eat(nc.sbuf_tensor("a_QT", [128, S], BF16))
        KT = eat(nc.sbuf_tensor("a_KT", [128, S], BF16))
        VT = eat(nc.sbuf_tensor("a_VT", [128, S], BF16))
        VN = eat(nc.sbuf_tensor("a_VN", [128, S], BF16))
        PT = [eat(nc.sbuf_tensor(f"a_PT{i}", [128, 512], BF16)) for i in range(4)]
        PTm = [eat(nc.sbuf_tensor(f"a_PTm{i}", [128, 512], BF16)) for i in range(4)]
        rl = eat(nc.sbuf_tensor("a_rl", [128, 512], F32))
        aost = [eat(nc.sbuf_tensor(f"a_ao{i}", [128, S], BF16)) for i in range(2)]

        def n_att_before(hb):
            return hb * 40

        for h in range(NH):
            @block.sync
            def _(sy):
                for w in range(3):
                    cnt = h * 3 + w
                    if cnt >= 2:
                        sy.wait_ge(sh["a_wtr"], NHT * (cnt - 1))
                    sy.dma_start(out=wst[cnt % 2][:],
                                 in_=wqkv_d[w * D + h * 128:w * D + h * 128 + 128, :]
                                 ).then_inc(sh[f"a_w{cnt % 2}"], 16)
                    scnt[f"a_w{cnt % 2}"] += 16

            @block.tensor
            def _(t):
                if h > 0:
                    t.wait_ge(sh["a_oev"], 2 * (h * BPC) * 4)
                for w in range(3):
                    cnt = h * 3 + w
                    t.wait_ge(sh[f"a_w{cnt % 2}"], 16 * (cnt // 2 + 1))
                    for j in range(NHT):
                        ev = NHT * cnt + j
                        if ev >= 2:
                            t.wait_ge(sh["a_wT"], ev - 1)
                        inc("a_wtr",
                            t.transpose(psF[4 + ev % 2][:, 0:128],
                                        wst[cnt % 2][:, 128 * j:128 * (j + 1)],
                                        C.ident_f32[:]), 1)

            @block.scalar
            def _(s):
                for w in range(3):
                    cnt = h * 3 + w
                    for j in range(NHT):
                        ev = NHT * cnt + j
                        s.wait_ge(sh["a_wtr"], ev + 1)
                        inc("a_wT",
                            s.copy(wqkvT[:, (j * 3 + w) * 128:(j * 3 + w) * 128 + 128],
                                   psF[4 + ev % 2][:, 0:128]), 1)

            for b in range(BPC):
                hb = h * BPC + b

                @block.tensor
                def _(t):
                    t.wait_ge(sh["a_wT"], 48 * (h + 1))
                    if hb > 0:
                        t.wait_ge(sh["a_pv"], n_att_before(hb))
                    for w in range(3):
                        t.wait_ge(sh["a_qev"], hb * 12 + 4 * w)
                        for ht in range(NHT):
                            for qb in range(4):
                                mm = t.matmul(
                                    psF[qb][:],
                                    wqkvT[:, (ht * 3 + w) * 128:(ht * 3 + w) * 128 + 128],
                                    xnT[ht][:, b * S + qb * 512:b * S + qb * 512 + 512],
                                    start=(ht == 0), stop=(ht == NHT - 1))
                                if ht == NHT - 1:
                                    inc("a_qkv", mm, 1)

                @block.scalar
                def _(s):
                    for w in range(3):
                        dstt = [QT, KT, VT][w]
                        for qb in range(4):
                            s.wait_ge(sh["a_qkv"], hb * 12 + w * 4 + qb + 1)
                            inc("a_qev",
                                s.copy(dstt[:, qb * 512:qb * 512 + 512],
                                       psF[qb][:]), 1)

                @block.tensor
                def _(t):
                    t.wait_ge(sh["a_qev"], hb * 12 + 12)
                    for kt in range(NHT):
                        ev = hb * NHT + kt
                        if ev >= 2:
                            t.wait_ge(sh["a_vn"], ev - 1)
                        inc("a_vtr",
                            t.transpose(ptV[ev % 2][:, 0:128],
                                        VT[:, 128 * kt:128 * kt + 128],
                                        C.ident_bf[:]), 1)

                @block.scalar
                def _(s):
                    for kt in range(NHT):
                        ev = hb * NHT + kt
                        s.wait_ge(sh["a_vtr"], ev + 1)
                        inc("a_vn",
                            s.copy(VN[:, 128 * kt:128 * kt + 128],
                                   ptV[ev % 2][:, 0:128]), 1)

                iS0 = n_att_before(hb)

                @block.tensor
                def _(t):
                    t.wait_ge(sh["a_vn"], hb * NHT + NHT)
                    iS = iS0
                    for qb in range(4):
                        nk = 4 * qb + 4

                        def emit_S(j, iS_j):
                            if iS_j >= 4:
                                t.wait_ge(sh["a_exp"], iS_j - 3)
                            inc("a_s",
                                t.matmul(psF[iS_j % 4][:],
                                         KT[:, 128 * j:128 * j + 128],
                                         QT[:, qb * 512:qb * 512 + 512],
                                         start=True, stop=True), 1)

                        for i in range(nk):
                            if i == 0:
                                for j in range(min(4, nk)):
                                    emit_S(j, iS + j)
                            t.wait_ge(sh["a_msk"], iS + 1)
                            if i == 0 and hb * 4 + qb > 0:
                                t.wait_ge(sh["a_oev"], 2 * (hb * 4 + qb) - 1)
                            t.matmul(psF[4][:], C.ones_bf[:], PTm[iS % 4][:],
                                     start=(i == 0), stop=(i == nk - 1))
                            if i == 0 and hb * 4 + qb > 0:
                                t.wait_ge(sh["a_oev"], 2 * (hb * 4 + qb))
                            inc("a_pv",
                                t.matmul(psF[5][:], VN[:, 128 * i:128 * i + 128],
                                         PTm[iS % 4][:],
                                         start=(i == 0), stop=(i == nk - 1)), 1)
                            if i + 4 < nk:
                                emit_S(i + 4, iS + 4)
                            iS += 1

                @block.scalar
                def _(s):
                    iS = iS0
                    for qb in range(4):
                        for i in range(4 * qb + 4):
                            s.wait_ge(sh["a_s"], iS + 1)
                            if iS >= 4:
                                s.wait_ge(sh["a_msk"], iS - 3)
                            inc("a_exp",
                                s.activation(PT[iS % 4][:], psF[iS % 4][:],
                                             AF.Exp, scale=INVSQ), 1)
                            iS += 1

                @block.gpsimd
                def _(g):
                    iS = iS0
                    for qb in range(4):
                        for i in range(4 * qb + 4):
                            g.wait_ge(sh["a_exp"], iS + 1)
                            if iS >= 4:
                                g.wait_ge(sh["a_pv"], iS - 3)
                            inc("a_msk",
                                g.affine_select(PTm[iS % 4][:], PT[iS % 4][:],
                                                pattern=[[1, 512]],
                                                compare_op=ALU.is_ge, fill=0.0,
                                                base=-128 * (i - 4 * qb),
                                                channel_multiplier=-1), 1)
                            iS += 1

                @block.vector
                def _(v):
                    if hb >= 2:
                        v.wait_ge(sh[f"a_ao{b}"], 16 * h)
                    for qb in range(4):
                        v.wait_ge(sh["a_pv"],
                                  iS0 + sum(4 * q + 4 for q in range(qb + 1)))
                        inc("a_oev", v.reciprocal(rl[:], psF[4][:]), 1)
                        v.wait_ge(sh["a_oev"], 2 * (hb * 4 + qb) + 1)
                        inc("a_oev",
                            v.tensor_tensor(aost[b][:, qb * 512:qb * 512 + 512],
                                            psF[5][:], rl[:], ALU.mult), 1)

                @block.scalar
                def _(s):
                    s.wait_ge(sh["a_oev"], 2 * (hb * 4 + 4))
                    inc(f"a_ao{b}",
                        s.dma_start(out=aoT_d[b, h, :, :], in_=aost[b][:]), 16)

        @block.scalar
        def _(s):
            s.wait_ge(sh["a_ao0"], 16 * NH)
            s.wait_ge(sh["a_ao1"], 16 * NH)

        assert scnt["a_w0"] + scnt["a_w1"] == 16 * NH * 3
        chk("a_wtr", NH * 3 * NHT)
        chk("a_wT", NH * 3 * NHT)
        chk("a_qkv", NH * BPC * 12)
        chk("a_qev", NH * BPC * 12)
        chk("a_vtr", NH * BPC * NHT)
        chk("a_vn", NH * BPC * NHT)
        chk("a_s", NH * BPC * 40)
        chk("a_exp", NH * BPC * 40)
        chk("a_msk", NH * BPC * 40)
        chk("a_pv", NH * BPC * 40)
        chk("a_oev", NH * BPC * 8)
        chk("a_ao0", 16 * NH)
        chk("a_ao1", 16 * NH)
    p1stack.close()


def _phase2a(C, ea, gpk_d, gsc_d, upk_d, usc_d, gl_d, up_d, xnT):
    nc, sh, scnt, block, inc, chk = C.nc, C.sh, C.scnt, C.block, C.inc, C.chk
    with contextlib.ExitStack() as p2a:
        e = p2a.enter_context
        pkg = [e(nc.sbuf_tensor(f"g_pk{i}", [128, 256], I32)) for i in range(2)]
        scgq = [e(nc.sbuf_tensor(f"g_scq{i}", [128, 1024], BF16))
                for i in range(2)]
        m32 = e(nc.sbuf_tensor("g_m32", [128, 256], I32))
        s32 = e(nc.sbuf_tensor("g_s32", [128, 256], I32))
        r1 = e(nc.sbuf_tensor("g_r1", [128, 512], BF16))
        sgn = e(nc.sbuf_tensor("g_sgn", [128, 512], BF16))
        tt1 = e(nc.sbuf_tensor("g_t1", [128, 512], BF16))
        w2 = [[e(nc.sbuf_tensor(f"w{p}_{m}", [128, 8192], BF16))
               for m in range(2)] for p in range(2)]
        glb = [e(nc.sbuf_tensor(f"g_gl{i}", [128, 512], BF16)) for i in range(3)]
        psGU = [e(nc.psum_tensor(f"g_ps{i}", [128, 512], F32)) for i in range(8)]

        for g in range(NGRP):
            for mat in range(2):
                li = 2 * g + mat

                @block.sync
                def _(sy):
                    for q in range(4):
                        gqi = 4 * li + q
                        if gqi >= 2:
                            sy.wait_ge(sh["g_ext"], 8 * (gqi - 1))
                        pks = "g_pk" if gqi % 2 == 0 else "g_pk1"
                        sy.dma_start(
                            out=pkg[gqi % 2][:],
                            in_=(gpk_d if mat == 0 else upk_d)[
                                g][:, q * 256:q * 256 + 256]
                        ).then_inc(sh[pks], 16)
                        scnt[pks] += 16

                @block.gpsimd
                def _(gq):
                    for q in range(4):
                        gqi = 4 * li + q
                        if gqi >= 2:
                            gq.wait_ge(sh["g_ch"], 20 * (gqi - 1))
                        srep = (gsc_d if mat == 0 else usc_d)[g][
                            :, None, 2 * q:2 * q + 2, :].broadcast_to(
                            [8, 16, 2, 512])
                        scs = "g_sc0" if gqi % 2 == 0 else "g_sc1"
                        gq.dma_start(out=scgq[gqi % 2][:], in_=srep
                                     ).then_inc(sh[scs], 16)
                        scnt[scs] += 16

                @block.vector
                def _(v):
                    if g >= 2:
                        v.wait_ge(sh["g_mm"], 64 * (g - 1))
                    for q in range(4):
                        gqi = 4 * li + q
                        v.wait_ge(sh["g_pk" if gqi % 2 == 0 else "g_pk1"],
                                  16 * (gqi // 2 + 1))
                        v.wait_ge(sh["g_sc0" if gqi % 2 == 0 else "g_sc1"],
                                  16 * (gqi // 2 + 1))
                        scv2 = scgq[gqi % 2][:].rearrange(
                            "p (jc x) -> p jc x", jc=2, x=512)
                        qsl = pkg[gqi % 2][:]
                        for si, (nm, msh, ssh, beta, nu) in enumerate(SUBSETS):
                            idx = li * 16 + q * 4 + si
                            v.tensor_scalar(m32[:], qsl, msh, 0x00070007,
                                            ALU.logical_shift_right,
                                            ALU.bitwise_and)
                            inc("g_ext",
                                v.tensor_scalar(s32[:], qsl, ssh, 0x00010001,
                                                ALU.logical_shift_right,
                                                ALU.bitwise_and), 2)
                            m16 = m32[:].bitcast(I16)
                            s16 = s32[:].bitcast(I16)
                            v.wait_ge(sh["g_ext"], 2 * idx + 2)
                            inc("g_ch",
                                v.tensor_scalar(r1[:], m16, 2.0, -8.0,
                                                ALU.mult, ALU.add), 1)
                            v.wait_ge(sh["g_ch"], 5 * idx + 1)
                            inc("g_ch",
                                v.scalar_tensor_tensor(tt1[:], m16, -2.0,
                                                       r1[:], ALU.add,
                                                       ALU.max), 1)
                            v.wait_ge(sh["g_ch"], 5 * idx + 2)
                            inc("g_ch",
                                v.scalar_tensor_tensor(r1[:], m16, 0.5,
                                                       tt1[:], ALU.mult,
                                                       ALU.max), 1)
                            inc("g_ch",
                                v.tensor_scalar(sgn[:], s16, -2.0, 1.0,
                                                ALU.mult, ALU.add), 1)
                            sg2v = sgn[:].rearrange("p (a b) -> p a b",
                                                    a=2, b=256)
                            scv = scv2[:, :, beta * 256:beta * 256 + 256]
                            v.wait_ge(sh["g_ch"], 5 * idx + 4)
                            inc("g_ch",
                                v.tensor_tensor(sg2v, sg2v, scv, ALU.mult), 1)
                            v.wait_ge(sh["g_ch"], 5 * idx + 5)
                            dst = w2[g % 2][mat][:].rearrange(
                                "p (ht k bb c) -> p ht bb k c",
                                ht=16, k=2, bb=2, c=128)
                            for j2 in range(2):
                                ht = 2 * (2 * q + j2) + nu
                                for k in range(2):
                                    o0 = j2 * 256 + k * 128
                                    inc("g_w0", v.tensor_tensor(
                                        dst[:, ht, beta, k, :],
                                        r1[:, o0:o0 + 128],
                                        sgn[:, o0:o0 + 128],
                                        ALU.mult), 1)

            for sr in range(4):
                gbase = (g * 4 + sr) * 8
                sig4 = g * 4 + sr
                sup, srel = divmod(sig4, 8)

                @block.tensor
                def _(t):
                    for half in range(2):
                        for mat in range(2):
                            t.wait_ge(sh["g_w0"], 64 * (2 * g + mat + 1))
                            ww = w2[g % 2][mat]
                            for ht in range(NHT):
                                for tq in range(4):
                                    tb = half * 4 + tq
                                    bank = mat * 4 + tq
                                    u = sig4 * 2 + half
                                    if ht == 0 and u >= 1:
                                        if mat == 0:
                                            t.wait_ge(sh["g_gelu"],
                                                      4 * (u - 1) + tq + 1)
                                        else:
                                            t.wait_ge(sh["g_upc"],
                                                      4 * (u - 1) + tq + 1)
                                    mm = t.matmul(
                                        psGU[bank][:],
                                        ww[:, (ht * 4 + sr) * 128:
                                           (ht * 4 + sr) * 128 + 128],
                                        xnT[ht][:, tb * 512:tb * 512 + 512],
                                        start=(ht == 0), stop=(ht == NHT - 1))
                                    if ht == NHT - 1:
                                        inc("g_mm", mm, 1)

                @block.scalar
                def _(s):
                    for half in range(2):
                        for mat in range(2):
                            for tq in range(4):
                                tb = half * 4 + tq
                                bank = mat * 4 + tq
                                nev = sig4 * 16 + half * 8 + mat * 4 + tq
                                s.wait_ge(sh["g_mm"], nev + 1)
                                if nev >= 3:
                                    s.wait_ge(sh[f"g_ist{nev % 3}"],
                                              16 * (nev // 3))
                                if mat == 0:
                                    inc("g_gelu",
                                        s.activation(glb[nev % 3][:],
                                                     psGU[bank][:],
                                                     AF.Gelu_apprx_tanh), 1)
                                    s.wait_ge(sh["g_gelu"],
                                              sig4 * 8 + half * 4 + tq + 1)
                                    inc(f"g_ist{nev % 3}",
                                        s.dma_start(
                                            out=gl_d[sup, tb, srel, :, :],
                                            in_=glb[nev % 3][:]), 16)
                                else:
                                    inc("g_upc",
                                        s.copy(glb[nev % 3][:],
                                               psGU[bank][:]), 1)
                                    s.wait_ge(sh["g_upc"],
                                              sig4 * 8 + half * 4 + tq + 1)
                                    inc(f"g_ist{nev % 3}",
                                        s.dma_start(
                                            out=up_d[sup, tb, srel, :, :],
                                            in_=glb[nev % 3][:]), 16)

        @block.scalar
        def _(s):
            for k in range(3):
                s.wait_ge(sh[f"g_ist{k}"], scnt[f"g_ist{k}"])
            s.wait_ge(sh["g_upc"], scnt["g_upc"])

        assert scnt["g_pk"] + scnt["g_pk1"] == 16 * NGRP * 8
        assert scnt["g_sc0"] + scnt["g_sc1"] == 16 * NGRP * 8
        chk("g_ext", NGRP * 2 * 32)
        chk("g_ch", NGRP * 2 * 80)
        chk("g_w0", NGRP * 2 * 64)
        chk("g_mm", NGRP * 64)
        chk("g_gelu", NGRP * 32)
        chk("g_upc", NGRP * 32)
        assert sum(scnt[f"g_ist{k}"] for k in range(3)) == 16 * NGRP * 64


def _phase2b(C, eb, dpk_d, dsc_d, gl_d, up_d, aoT_d, wout_d, acc):
    nc, sh, scnt, block, inc, chk = C.nc, C.sh, C.scnt, C.block, C.inc, C.chk

    with contextlib.ExitStack() as prj:
        ep_ = prj.enter_context
        wst = [ep_(nc.sbuf_tensor(f"i_wst{i}", [128, D], F32)) for i in range(2)]
        woT = [ep_(nc.sbuf_tensor(f"i_woT{j}", [128, D], BF16)) for j in range(8)]
        aosr = [ep_(nc.sbuf_tensor(f"i_ao{i}", [128, 8 * 128], BF16))
                for i in range(2)]
        psJ = [ep_(nc.psum_tensor(f"i_ps{i}", [128, 512], F32)) for i in range(8)]

        for half in range(2):
            @block.sync
            def _(sy):
                for et in range(NHT):
                    cnt = half * NHT + et
                    if cnt >= 2:
                        sy.wait_ge(sh["i_wtr"], 8 * (cnt - 1))
                    sy.dma_start(out=wst[cnt % 2][:],
                                 in_=wout_d[et * 128:et * 128 + 128, :]
                                 ).then_inc(sh[f"i_wo{cnt % 2}"], 16)
                    scnt[f"i_wo{cnt % 2}"] += 16

            @block.tensor
            def _(t):
                if half == 1:
                    t.wait_ge(sh["i_ev"], 4 * 32)
                for et in range(NHT):
                    cnt = half * NHT + et
                    t.wait_ge(sh[f"i_wo{cnt % 2}"], 16 * (cnt // 2 + 1))
                    for jl in range(8):
                        ev = 8 * cnt + jl
                        if ev >= 2:
                            t.wait_ge(sh["i_woT"], ev - 1)
                        inc("i_wtr",
                            t.transpose(psJ[ev % 2][:, 0:128],
                                        wst[cnt % 2][:, (half * 8 + jl) * 128:
                                                      (half * 8 + jl) * 128 + 128],
                                        C.ident_f32[:]), 1)

            @block.scalar
            def _(s):
                for et in range(NHT):
                    cnt = half * NHT + et
                    for jl in range(8):
                        ev = 8 * cnt + jl
                        s.wait_ge(sh["i_wtr"], ev + 1)
                        inc("i_woT",
                            s.copy(woT[jl][:, et * 128:et * 128 + 128],
                                   psJ[ev % 2][:, 0:128]), 1)

            for tt in range(NTT):
                li = half * NTT + tt
                b, ts_ = tt // 16, (tt % 16) * 128

                @block.sync
                def _(sy):
                    if li >= 2:
                        sy.wait_ge(sh["i_mm"], li - 1)
                    sy.dma_start(
                        out=aosr[li % 2][:],
                        in_=aoT_d[b, half * 8:half * 8 + 8, :, ts_:ts_ + 128]
                        .rearrange("hh p c -> p hh c")
                    ).then_inc(sh[f"i_ao{li % 2}"], 16)
                    scnt[f"i_ao{li % 2}"] += 16

                pset = [psJ[4 * (tt % 2) + ch] for ch in range(4)]

                @block.tensor
                def _(t):
                    if tt == 0:
                        t.wait_ge(sh["i_woT"], 128 * (half + 1))
                    t.wait_ge(sh[f"i_ao{li % 2}"], 16 * (li // 2 + 1))
                    if li >= 2:
                        t.wait_ge(sh["i_ev"], 4 * (li - 1))
                    for dtl in range(8):
                        for ch in range(4):
                            mm = t.matmul(
                                pset[ch][:],
                                aosr[li % 2][:, dtl * 128:dtl * 128 + 128],
                                woT[dtl][:, ch * 512:ch * 512 + 512],
                                start=(dtl == 0), stop=(dtl == 7))
                    inc("i_mm", mm, 1)

                if half == 0:
                    @block.scalar
                    def _(s):
                        s.wait_ge(sh["i_mm"], li + 1)
                        for ch in range(4):
                            inc("i_ev",
                                s.copy(acc[tt][:, ch * 512:ch * 512 + 512],
                                       pset[ch][:]), 1)
                else:
                    @block.vector
                    def _(v):
                        v.wait_ge(sh["i_mm"], li + 1)
                        for ch in range(4):
                            inc("i_ev", v.tensor_tensor(
                                acc[tt][:, ch * 512:ch * 512 + 512],
                                pset[ch][:],
                                acc[tt][:, ch * 512:ch * 512 + 512],
                                ALU.add), 1)

        assert scnt["i_wo0"] + scnt["i_wo1"] == 16 * 2 * NHT
        chk("i_wtr", 2 * NHT * 8)
        chk("i_woT", 2 * NHT * 8)
        assert scnt["i_ao0"] + scnt["i_ao1"] == 16 * 2 * NTT
        chk("i_mm", 2 * NTT)
        chk("i_ev", 2 * NTT * 4)

    with contextlib.ExitStack() as p2b:
        e = p2b.enter_context
        for en in ("sync", "gpsimd"):
            getattr(block, en)(lambda q: q.wait_ge(sh["i_ev"], 2 * NTT * 4))
        pkd = e(nc.sbuf_tensor("d_pkb", [128, 512], I32))
        scb = e(nc.sbuf_tensor("d_scb", [128, 2048], BF16))
        m32 = e(nc.sbuf_tensor("d_m32", [128, 256], I32))
        s32 = e(nc.sbuf_tensor("d_s32", [128, 256], I32))
        r1 = e(nc.sbuf_tensor("d_r1", [128, 512], BF16))
        sgn = e(nc.sbuf_tensor("d_sgn", [128, 512], BF16))
        tt1 = e(nc.sbuf_tensor("d_t1", [128, 512], BF16))
        wd = [e(nc.sbuf_tensor(f"d_wd{i}", [128, 2048], BF16))
              for i in range(15)]
        glt = [e(nc.sbuf_tensor(f"d_gt{i}", [128, 1024], BF16)) for i in range(2)]
        upt = e(nc.sbuf_tensor("d_ut", [128, 1024], BF16))
        psD = [e(nc.psum_tensor(f"d_ps{i}", [128, 512], F32)) for i in range(8)]

        DORD = ([(jc, q2, si) for jc in range(3) for q2 in range(2)
                 for si in range(4)]
                + [(3, q2, si) for q2 in range(2) for si in (0, 1)]
                + [(3, q2, si) for q2 in range(2) for si in (2, 3)])
        assert len(DORD) == 32

        def d_loads(sup, jc):
            J = sup * 4 + jc

            @block.sync
            def _(sy):
                if J >= 1:
                    sy.wait_ge(sh["d_ext"], 16 * J)
                sy.dma_start(out=pkd[:], in_=dpk_d[sup, jc]
                             ).then_inc(sh["d_pk0"], 16)
            scnt["d_pk0"] += 16

            @block.gpsimd
            def _(gq):
                if J >= 1:
                    gq.wait_ge(sh["d_ch"], 40 * J)
                srep = dsc_d[sup, :, jc, :][:, None, :].broadcast_to(
                    [8, 16, 2048])
                gq.dma_start(out=scb[:], in_=srep).then_inc(sh["d_sc0"], 16)
            scnt["d_sc0"] += 16

        def d_deq(sup, pos):
            jc, q2, si = DORD[pos]
            J = sup * 4 + jc
            idx = sup * 32 + pos
            nm, msh, ssh, beta, nu = SUBSETS[si]
            srel = 2 * jc + nu
            slot = (8 * sup + srel) % 15

            @block.vector
            def _(v):
                v.wait_ge(sh["d_pk0"], 16 * (J + 1))
                v.wait_ge(sh["d_sc0"], 16 * (J + 1))
                qsl = pkd[:, q2 * 256:q2 * 256 + 256]
                v.tensor_scalar(m32[:], qsl, msh, 0x00070007,
                                ALU.logical_shift_right, ALU.bitwise_and)
                inc("d_ext",
                    v.tensor_scalar(s32[:], qsl, ssh, 0x00010001,
                                    ALU.logical_shift_right,
                                    ALU.bitwise_and), 2)
                m16 = m32[:].bitcast(I16)
                s16 = s32[:].bitcast(I16)
                v.wait_ge(sh["d_ext"], 2 * idx + 2)
                inc("d_ch",
                    v.tensor_scalar(r1[:], m16, 2.0, -8.0,
                                    ALU.mult, ALU.add), 1)
                v.wait_ge(sh["d_ch"], 5 * idx + 1)
                inc("d_ch",
                    v.scalar_tensor_tensor(tt1[:], m16, -2.0, r1[:],
                                           ALU.add, ALU.max), 1)
                v.wait_ge(sh["d_ch"], 5 * idx + 2)
                inc("d_ch",
                    v.scalar_tensor_tensor(r1[:], m16, 0.5, tt1[:],
                                           ALU.mult, ALU.max), 1)
                inc("d_ch",
                    v.tensor_scalar(sgn[:], s16, -2.0, 1.0,
                                    ALU.mult, ALU.add), 1)
                scv = scb[:, beta * 1024 + q2 * 512:
                          beta * 1024 + q2 * 512 + 512]
                v.wait_ge(sh["d_ch"], 5 * idx + 4)
                inc("d_ch", v.tensor_tensor(sgn[:], sgn[:], scv, ALU.mult), 1)
                v.wait_ge(sh["d_ch"], 5 * idx + 5)
                if srel == 7 and sup >= 1:
                    v.wait_ge(sh["d_mm"], 32 * sup)
                elif sup >= 2 and pos == 0:
                    v.wait_ge(sh["d_mm"], 32 * (sup - 1))
                dst = wd[slot][:].rearrange(
                    "p (q j par c) -> p q j par c", q=2, j=4, par=2, c=128)
                inc("d_w", v.tensor_tensor(
                    dst[:, q2, :, beta, :],
                    r1[:].rearrange("p (j c) -> p j c", j=4, c=128),
                    sgn[:].rearrange("p (j c) -> p j c", j=4, c=128),
                    ALU.mult), 1)

        for jc in range(4):
            d_loads(0, jc)
            for pos in range(8 * jc, 8 * jc + 8):
                d_deq(0, pos)

        for sup in range(NSUP):
            for tt in range(NTT):
                li2 = sup * NTT + tt
                tb, c0 = tt // 4, (tt % 4) * 128

                @block.sync
                def _(sy):
                    if li2 >= 2:
                        sy.wait_ge(sh["d_mm"], li2 - 1)
                    sy.dma_start(
                        out=glt[li2 % 2][:],
                        in_=gl_d[sup, tb, :, :, c0:c0 + 128].rearrange(
                            "ss p c -> p ss c")
                    ).then_inc(sh[f"d_il{li2 % 2}"], 16)
                    scnt[f"d_il{li2 % 2}"] += 16
                    if li2 >= 1:
                        sy.wait_ge(sh["d_it"], li2)
                    sy.dma_start(
                        out=upt[:],
                        in_=up_d[sup, tb, :, :, c0:c0 + 128].rearrange(
                            "ss p c -> p ss c")
                    ).then_inc(sh["d_il2"], 16)
                    scnt["d_il2"] += 16


                pset = [psD[4 * (tt % 2) + ch] for ch in range(4)]

                @block.tensor
                def _(t):
                    if tt == 0:
                        t.wait_ge(sh["d_w"], 32 * sup + 28)
                    t.wait_ge(sh["d_it"], li2 + 1)
                    if li2 >= 2:
                        t.wait_ge(sh["d_acc"], 4 * (li2 - 1))
                    for srel in range(8):
                        if tt == 0 and srel == 7:
                            t.wait_ge(sh["d_w"], 32 * (sup + 1))
                        for ch in range(4):
                            mm = t.matmul(
                                pset[ch][:],
                                glt[li2 % 2][:, srel * 128:srel * 128 + 128],
                                wd[(8 * sup + srel) % 15][:,
                                   ch * 512:ch * 512 + 512],
                                start=(srel == 0), stop=(srel == 7))
                    inc("d_mm", mm, 1)

                @block.vector
                def _(v):
                    v.wait_ge(sh[f"d_il{li2 % 2}"], 16 * (li2 // 2 + 1))
                    v.wait_ge(sh["d_il2"], 16 * (li2 + 1))
                    inc("d_it",
                        v.tensor_tensor(glt[li2 % 2][:], glt[li2 % 2][:],
                                        upt[:], ALU.mult), 1)
                    if li2 >= 1:
                        v.wait_ge(sh["d_mm"], li2)
                        tp = (li2 - 1) % NTT
                        ppre = [psD[4 * (tp % 2) + ch] for ch in range(4)]
                        for ch in range(4):
                            inc("d_acc", v.tensor_tensor(
                                acc[tp][:, ch * 512:ch * 512 + 512],
                                ppre[ch][:],
                                acc[tp][:, ch * 512:ch * 512 + 512],
                                ALU.add), 1)

                if sup + 1 < NSUP and tt % 8 == 0:
                    d_loads(sup + 1, tt // 8)
                if sup + 1 < NSUP and tt < 28:
                    d_deq(sup + 1, tt)

            if sup + 1 < NSUP:
                for pos in range(28, 32):
                    d_deq(sup + 1, pos)

        @block.vector
        def _(v):
            v.wait_ge(sh["d_mm"], NSUP * NTT)
            plast = [psD[4 * ((NTT - 1) % 2) + ch] for ch in range(4)]
            for ch in range(4):
                inc("d_acc", v.tensor_tensor(
                    acc[NTT - 1][:, ch * 512:ch * 512 + 512],
                    plast[ch][:],
                    acc[NTT - 1][:, ch * 512:ch * 512 + 512],
                    ALU.add), 1)


        chk("d_pk0", 16 * NSUP * 4)
        chk("d_sc0", 16 * NSUP * 4)
        chk("d_ext", NSUP * 64)
        chk("d_ch", NSUP * 160)
        chk("d_w", NSUP * 32)
        assert scnt["d_il0"] + scnt["d_il1"] == 16 * NSUP * NTT
        chk("d_il2", 16 * NSUP * NTT)
        chk("d_it", NSUP * NTT)
        chk("d_mm", NSUP * NTT)
        chk("d_acc", NSUP * NTT * 4)


def _phase3(C, eb, x_d, y_d, acc):
    nc, sh, scnt, block, inc, chk = C.nc, C.sh, C.scnt, C.block, C.inc, C.chk
    with contextlib.ExitStack() as p3:
        e = p3.enter_context
        xst = [e(nc.sbuf_tensor(f"p3_x{i}", [128, D], F32)) for i in range(2)]
        ost = [e(nc.sbuf_tensor(f"p3_o{i}", [128, D], F32)) for i in range(2)]

        for i in range(NTT):
            @block.sync
            def _(sy):
                if i >= 2:
                    sy.wait_ge(sh["p3_out"], i - 1)
                sy.dma_start(out=xst[i % 2][:], in_=x_d[128 * i:128 * (i + 1), :]
                             ).then_inc(sh[f"p3_x{i % 2}"], 16)
            scnt[f"p3_x{i % 2}"] += 16

            @block.vector
            def _(v):
                v.wait_ge(sh[f"p3_x{i % 2}"], 16 * (i // 2 + 1))
                if i >= 2:
                    v.wait_ge(sh[f"p3_st{i % 2}"], 16 * (i // 2))
                av = acc[i][:].rearrange("p (blk par pair) -> p blk pair par",
                                         blk=8, par=2, pair=128)
                xv = xst[i % 2][:].rearrange("p (blk pair par) -> p blk pair par",
                                             blk=8, pair=128, par=2)
                ov = ost[i % 2][:].rearrange("p (blk pair par) -> p blk pair par",
                                             blk=8, pair=128, par=2)
                inc("p3_out", v.tensor_tensor(ov, av, xv, ALU.add), 1)

            @block.scalar
            def _(s):
                s.wait_ge(sh["p3_out"], i + 1)
                inc(f"p3_st{i % 2}",
                    s.dma_start(out=y_d[128 * i:128 * (i + 1), :],
                                in_=ost[i % 2][:]), 16)

        @block.scalar
        def _(s):
            s.wait_ge(sh["p3_st0"], scnt["p3_st0"])
            s.wait_ge(sh["p3_st1"], scnt["p3_st1"])

        assert scnt["p3_x0"] + scnt["p3_x1"] == 16 * NTT
        chk("p3_out", NTT)
        assert scnt["p3_st0"] + scnt["p3_st1"] == 16 * NTT


_CACHE = {}


def kernel(**inputs):
    in_maps = host_prep(inputs)
    if "nc" not in _CACHE:
        _CACHE["nc"] = build_program(upto_phase=3)
    nc = _CACHE["nc"]
    res = run_bass_kernel_spmd(nc, in_maps, core_ids=list(range(NCORES)))
    out = np.empty((B, S, D), dtype=np.float32)
    for c in range(NCORES):
        out[BPC * c: BPC * (c + 1)] = np.asarray(
            res.results[c]["y"]).reshape(BPC, S, D)
    return out
